# version 13
# baseline (speedup 1.0000x reference)
import math
from contextlib import ExitStack

import numpy as np
import ml_dtypes

import concourse.bass as bass
import concourse.mybir as mybir
from concourse.bass_utils import run_bass_kernel_spmd

F32 = mybir.dt.float32
BF16 = mybir.dt.bfloat16
AF = mybir.ActivationFunctionType
ALU = mybir.AluOpType
AX = mybir.AxisListType
bfdt = ml_dtypes.bfloat16

NCORES = 8
SEQ = 2048
D = 2048
DC = D // 128
NH = 16
I = 8192
IC = I // 128
TQ = 512
NTQ = SEQ // TQ
TKC = SEQ // 128
FT = 1024
NFT = 2 * SEQ // FT
IB = 512
NIB = I // IB
IBC = IB // 128
RS128 = 1.0 / math.sqrt(128.0)
LNEPS = 1e-5

DEBUG_TAPS = ()
STOP_AFTER = None
DEBUG_FFN = None


def hadamard(n):
    idx = np.arange(n)
    a = idx[:, None] & idx[None, :]
    pc = np.zeros((n, n), dtype=np.int64)
    v = a.copy()
    while v.any():
        pc += v & 1
        v >>= 1
    return np.where(pc % 2 == 0, 1.0, -1.0).astype(np.float32)


class Sem:
    def __init__(self, handle):
        self.h = handle
        self.n = 0


CHAIN_PASS = {"wait_ge", "dma_start", "nop", "engine_nop", "dma_start_transpose"}


class Chained:

    def __init__(self, raw, sem):
        self._raw = raw
        self._sem = sem

    def wait_ge(self, h, v):
        return self._raw.wait_ge(h, v)

    def dma_start(self, *a, **k):
        return self._raw.dma_start(*a, **k)

    def __getattr__(self, name):
        f = getattr(self._raw, name)
        if name.startswith("_") or not callable(f) or name in CHAIN_PASS:
            return f
        sem = self._sem
        raw = self._raw

        def wrapped(*a, **k):
            raw.wait_ge(sem.h, sem.n)
            ins = f(*a, **k)
            ins.then_inc(sem.h, 1)
            sem.n += 1
            return ins

        return wrapped


class Build:
    def __init__(self):
        self.nc = bass.Bass("TRN2", target_bir_lowering=False,
                    detect_race_conditions=True)
        self.taps = {}

    def sec(self, engine_name):
        deco = getattr(self.block, engine_name)
        if engine_name == "vector":
            return lambda fn: deco(lambda eng: fn(Chained(eng, self.ve)))
        if engine_name == "scalar":
            return lambda fn: deco(lambda eng: fn(Chained(eng, self.sc)))
        if engine_name == "gpsimd":
            return lambda fn: deco(lambda eng: fn(Chained(eng, self.gp)))
        return deco

    def build(self):
        nc = self.nc
        ein, eout = "ExternalInput", "ExternalOutput"
        self.x = nc.dram_tensor("x", [2, SEQ, D], F32, kind=ein).ap()
        self.wp = nc.dram_tensor("wp", [3 * D, D], F32, kind=ein).ap()
        self.wo = nc.dram_tensor("wo", [D, D], F32, kind=ein).ap()
        self.wup = nc.dram_tensor("wup", [I, D], F32, kind=ein).ap()
        self.wgt = nc.dram_tensor("wgt", [I, D], F32, kind=ein).ap()
        self.wdn = nc.dram_tensor("wdn", [D, I], F32, kind=ein).ap()
        self.su_ug = nc.dram_tensor("su_ug", [128, DC], F32, kind=ein).ap()
        self.su_dp = nc.dram_tensor("su_dp", [128, IC], F32, kind=ein).ap()
        self.ws_ug = nc.dram_tensor("ws_ug", [128, 2 * IC], F32, kind=ein).ap()
        self.ws_dp = nc.dram_tensor("ws_dp", [128, DC], F32, kind=ein).ap()
        self.h128 = nc.dram_tensor("h128", [128, 128], BF16, kind=ein).ap()
        self.ident = nc.dram_tensor("ident", [128, 128], BF16, kind=ein).ap()
        self.masks = nc.dram_tensor("masks", [128, NTQ, TQ], BF16, kind=ein).ap()
        self.out = nc.dram_tensor("out", [2, SEQ, D], F32, kind=eout).ap()

        dram = lambda name, shape, dt: nc.dram_tensor(name, shape, dt, kind="Internal").ap()
        self.x_bf = dram("x_bf", [2, SEQ, D], BF16)
        self.wp_bf = dram("wp_bf", [NH, 3, 128, D], BF16)
        self.wo_bf = dram("wo_bf", [D, D], BF16)
        self.wup_bf = dram("wup_bf", [I, D], BF16)
        self.wgt_bf = dram("wgt_bf", [I, D], BF16)
        self.wd3_bf = dram("wd3_bf", [D, I], BF16)
        self.wpT_s = dram("wpT_s", [DC, 128, 3 * D], BF16)
        self.woT_s = dram("woT_s", [DC, 128, D], BF16)
        self.wupT_s = dram("wupT_s", [DC, 128, I], BF16)
        self.wgtT_s = dram("wgtT_s", [DC, 128, I], BF16)
        self.wd3T_s = dram("wd3T_s", [IC, 128, D], BF16)
        self.o_s = dram("o_s", [D, SEQ], BF16)
        self.y_s = dram("y_s", [2 * SEQ, D], F32)
        self.y_bf = dram("y_bf", [2 * SEQ, D], BF16)

        for t in DEBUG_TAPS:
            shape = {"os": [D, SEQ], "y": [2 * SEQ, D]}[t]
            self.taps[t] = nc.dram_tensor("tap_" + t, shape, F32, kind=eout).ap()

        with ExitStack() as ctx:
            en = ctx.enter_context
            sbt = lambda name, shape, dt: en(nc.sbuf_tensor(name, shape, dt))
            self.dL = Sem(en(nc.semaphore("dL")))
            self.dG = Sem(en(nc.semaphore("dG")))
            self.dS = Sem(en(nc.semaphore("dS")))
            self.pe = Sem(en(nc.semaphore("pe")))
            self.gp = Sem(en(nc.semaphore("gp")))
            self.sc = Sem(en(nc.semaphore("sc")))
            self.ve = Sem(en(nc.semaphore("ve")))

            self.c_h128 = sbt("c_h128", [128, 128], BF16)
            self.c_id = sbt("c_id", [128, 128], BF16)
            self.c_mask = sbt("c_mask", [128, NTQ, TQ], BF16)
            self.c_ones_b = sbt("c_ones_b", [128, 1], BF16)
            self.c_ones_f = sbt("c_ones_f", [1, 128], F32)
            self.c_eps = sbt("c_eps", [128, 1], F32)
            self.v_sup = sbt("v_sup", [128, IC], F32)
            self.v_sgt = sbt("v_sgt", [128, IC], F32)
            self.v_sdn = sbt("v_sdn", [128, DC], F32)
            self.v_suug = sbt("v_suug", [128, DC], F32)
            self.v_tmp = sbt("v_tmp", [128, IC], F32)

            self.ps = en(nc.psum_tensor("ps", [128, 7, 512], F32))
            self.ptr = en(nc.psum_tensor("ptr", [128, 512], BF16))

            self.block = en(nc.Block())
            self._consts()
            with ExitStack() as pctx:
                psb = lambda n_, s_, d_: pctx.enter_context(nc.sbuf_tensor(n_, s_, d_))
                self.p_wa = psb("p_wa", [128, I // 2], F32)
                self.p_wb = psb("p_wb", [128, I // 2], F32)
                self.p_wo = psb("p_wo", [128, I // 2], BF16)
                self.p_land = psb("p_land", [128, I], BF16)
                self._prep()
                self._prep_passb_build()
                for s in range(2):
                    with ExitStack() as actx:
                        asb = lambda n_, s_, d_: actx.enter_context(
                            nc.sbuf_tensor(n_, s_, d_))
                        self._attn_seq(s, asb)
                    with ExitStack() as wctx:
                        wsb = lambda n_, s_, d_: wctx.enter_context(
                            nc.sbuf_tensor(n_, s_, d_))
                        self._wout_seq(s, wsb)
            self._drain_passb(10 ** 9)
            if STOP_AFTER is None:
                with ExitStack() as fctx:
                    fsb = lambda n_, s_, d_: fctx.enter_context(
                        nc.sbuf_tensor(n_, s_, d_))
                    self._ffn(fsb)
            self._finish()
        return nc

    def _consts(self):
        dL, ve = self.dL, self.ve

        @self.sec("sync")
        def _(sync):
            sync.wait_ge(dL.h, dL.n)
            sync.dma_start(out=self.c_h128[:], in_=self.h128).then_inc(dL.h, 16)
            sync.dma_start(out=self.c_id[:], in_=self.ident).then_inc(dL.h, 16)
            sync.dma_start(out=self.c_mask[:], in_=self.masks).then_inc(dL.h, 16)
            sync.dma_start(out=self.v_suug[:], in_=self.su_ug).then_inc(dL.h, 16)
            sync.dma_start(out=self.v_sup[:], in_=self.su_dp).then_inc(dL.h, 16)
            sync.dma_start(out=self.v_sgt[:], in_=self.ws_ug[:, IC:]).then_inc(dL.h, 16)
            sync.dma_start(out=self.v_sdn[:], in_=self.ws_dp).then_inc(dL.h, 16)
            sync.dma_start(out=self.v_tmp[:], in_=self.ws_ug[:, :IC]).then_inc(dL.h, 16)
        dL.n += 8 * 16
        t_loads = dL.n

        @self.sec("vector")
        def _(vector):
            vector.wait_ge(dL.h, t_loads)
            vector.memset(self.c_ones_b[:], 1.0)
            vector.memset(self.c_ones_f[:], 1.0)
            vector.memset(self.c_eps[:], LNEPS)
            vector.tensor_scalar(self.v_sup[:], self.v_sup[:],
                                 1.0 / math.sqrt(2048.0), None, ALU.mult)
            vector.tensor_tensor(self.v_sup[:], self.v_sup[:], self.v_tmp[:], ALU.mult)
            vector.tensor_scalar(self.v_sgt[:], self.v_sgt[:],
                                 1.0 / math.sqrt(2048.0), None, ALU.mult)
            vector.tensor_scalar(self.v_sdn[:], self.v_sdn[:],
                                 1.0 / math.sqrt(8192.0), None, ALU.mult)
        self.t_consts_v = ve.n

    def _prep(self):
        dG, dS, ve = self.dG, self.dS, self.ve

        @self.sec("gpsimd")
        def _(g):
            base = dG.n
            wpr = self.wp.rearrange("(v hh r) d -> v hh r d", v=3, hh=NH)
            for v in range(3):
                g.wait_ge(dG.h, base)
                g.dma_start(out=self.wp_bf[:, v], in_=wpr[v]).then_inc(dG.h, 16)
                base += 16
            for s in range(2):
                g.wait_ge(dG.h, base)
                g.dma_start(out=self.x_bf[s], in_=self.x[s]).then_inc(dG.h, 16)
                base += 16
            for src, dst, nsplit in ((self.wo, self.wo_bf, 2),
                                     (self.wup, self.wup_bf, 4),
                                     (self.wgt, self.wgt_bf, 4)):
                rows = src.shape[0]
                step = rows // nsplit
                for k in range(nsplit):
                    g.wait_ge(dG.h, base)
                    g.dma_start(out=dst[k * step:(k + 1) * step],
                                in_=src[k * step:(k + 1) * step]).then_inc(dG.h, 16)
                    base += 16
        dG.n += 16 * 15
        self.t_cast_x = [16 * 4, 16 * 5]
        self.t_cast_wp = 16 * 3
        self.t_cast_wo = 16 * 7
        self.t_cast_wup = 16 * 11
        self.t_cast_wgt = 16 * 15

        self._wd_half = I // 2
        self._wd_prev_bfy = 0
        self._wd_prev_storeB = None
        self.t_wd_chunk = []
        self.t_wd_done = None

    def _prep_wd_chunk(self, c):
        dG, dS, ve = self.dG, self.dS, self.ve
        half = self._wd_half

        def p_load():
            prev_bfy = self._wd_prev_bfy
            t0 = dS.n

            @self.sec("scalar")
            def _(scal, c=c, prev_bfy=prev_bfy):
                scal.wait_ge(dS.h, dS.n)
                if prev_bfy > 0:
                    scal.wait_ge(ve.h, prev_bfy)
                scal.dma_start(out=self.p_wa[:],
                               in_=self.wdn[c * 128:(c + 1) * 128, 0:half]).then_inc(dS.h, 16)
                scal.dma_start(out=self.p_wb[:],
                               in_=self.wdn[c * 128:(c + 1) * 128, half:I]).then_inc(dS.h, 16)
            dS.n += 32
            self._wd_t_ld = dS.n

        def p1():
            t_ld = self._wd_t_ld

            @self.sec("vector")
            def _(v, c=c, t_ld=t_ld):
                v.wait_ge(dS.h, t_ld)
                sv = self.v_sdn[:, c:c + 1]
                v.tensor_scalar(self.p_wa[:], self.p_wa[:], sv, None, ALU.mult)
                v.tensor_scalar(self.p_wb[:], self.p_wb[:], sv, None, ALU.mult)
                v.tensor_add(self.p_wa[:], self.p_wa[:], self.p_wb[:])
                v.scalar_tensor_tensor(self.p_wb[:], self.p_wb[:], -2.0,
                                       self.p_wa[:], ALU.mult, ALU.add)

        def mk_bfy(buf):
            def p_bfy():
                @self.sec("vector")
                def _(v, buf=buf):
                    for stc in (16, 8, 4, 2, 1):
                        seg = stc * 128
                        for g0 in range(0, half, 2 * seg):
                            av = buf[:, g0:g0 + seg]
                            bv = buf[:, g0 + seg:g0 + 2 * seg]
                            v.tensor_add(av, av, bv)
                            v.scalar_tensor_tensor(bv, bv, -2.0, av,
                                                   ALU.mult, ALU.add)
            return p_bfy

        def p4():
            prev_storeB = self._wd_prev_storeB

            @self.sec("vector")
            def _(v, prev_storeB=prev_storeB):
                if prev_storeB is not None:
                    v.wait_ge(dG.h, prev_storeB)
                v.tensor_copy(self.p_wo[:], self.p_wa[:])
            t_castA = ve.n

            @self.sec("gpsimd")
            def _(g, c=c, t_castA=t_castA):
                g.wait_ge(dG.h, dG.n)
                g.wait_ge(ve.h, t_castA)
                g.dma_start(out=self.wd3_bf[c * 128:(c + 1) * 128, 0:half],
                            in_=self.p_wo[:]).then_inc(dG.h, 16)
            dG.n += 16
            self._wd_storeA = dG.n

        def p5():
            t_storeA = self._wd_storeA

            @self.sec("vector")
            def _(v, t_storeA=t_storeA):
                v.wait_ge(dG.h, t_storeA)
                v.tensor_copy(self.p_wo[:], self.p_wb[:])
            t_castB = ve.n
            self._wd_prev_bfy = ve.n

            @self.sec("gpsimd")
            def _(g, c=c, t_castB=t_castB):
                g.wait_ge(dG.h, dG.n)
                g.wait_ge(ve.h, t_castB)
                g.dma_start(out=self.wd3_bf[c * 128:(c + 1) * 128, half:I],
                            in_=self.p_wo[:]).then_inc(dG.h, 16)
            dG.n += 16
            self._wd_prev_storeB = dG.n
            self.t_wd_chunk.append(dG.n)
            if c == DC - 1:
                self.t_wd_done = dG.n

        return [p_load, p1, mk_bfy(self.p_wa), mk_bfy(self.p_wb), p4, p5]

    def _prep_passb_build(self):
        dL, dG = self.dL, self.dG
        self.t_scratch = {}
        self._passb = []
        self._land_free = 0

        def item(name, src_bf, dst_s, c, cols, t_src_dG, last):
            def emit():
                tr0 = dL.n

                @self.sec("sync")
                def _(sync):
                    sync.wait_ge(dL.h, dL.n)
                    if self._land_free > 0:
                        sync.wait_ge(dG.h, self._land_free)
                    sync.wait_ge(dG.h, self.t_wd_done if t_src_dG is None else t_src_dG)
                    sync.dma_start(out=self.p_land[:, 0:cols],
                                   in_=src_bf[:, c * 128:(c + 1) * 128],
                                   transpose=True).then_inc(dL.h, 16)
                dL.n += 16
                t_tr = dL.n

                @self.sec("gpsimd")
                def _(g, t_tr=t_tr):
                    g.wait_ge(dG.h, dG.n)
                    g.wait_ge(dL.h, t_tr)
                    g.dma_start(out=dst_s[c], in_=self.p_land[:, 0:cols]).then_inc(dG.h, 16)
                dG.n += 16
                self._land_free = dG.n
                if last:
                    self.t_scratch[name] = dG.n
            return emit

        wp_rows = self.wp_bf.rearrange("h v r d -> (h v r) d")
        for c in range(DC):
            self._passb.append(item("wp", wp_rows, self.wpT_s, c, 3 * D,
                                    self.t_cast_wp, c == DC - 1))
        for c in range(DC):
            self._passb.append(item("wo", self.wo_bf, self.woT_s, c, D,
                                    self.t_cast_wo, c == DC - 1))
        for c in range(DC):
            self._passb.append(item("wup", self.wup_bf, self.wupT_s, c, I,
                                    self.t_cast_wup, c == DC - 1))
        for c in range(DC):
            self._passb.append(item("wgt", self.wgt_bf, self.wgtT_s, c, I,
                                    self.t_cast_wgt, c == DC - 1))
        for c in range(IC):
            self._passb.append(item("wd3", self.wd3_bf, self.wd3T_s, c, D,
                                    None, c == IC - 1))


    def _drain_passb(self, k):
        for _ in range(min(k, len(self._passb))):
            self._passb.pop(0)()

    def _attn_seq(self, s, sbt):
        dL, dG, pe, sc, ve = self.dL, self.dG, self.pe, self.sc, self.ve
        ps, ptr = self.ps, self.ptr

        s_xT = sbt("s_xT%d" % s, [128, DC, SEQ], BF16)
        s_wph = sbt("s_wph%d" % s, [128, 2, DC, 3, 128], BF16)
        s_q = sbt("s_q%d" % s, [128, 2, SEQ], BF16)
        s_k = sbt("s_k%d" % s, [128, 2, SEQ], BF16)
        s_vT = sbt("s_vT%d" % s, [128, SEQ], BF16)
        s_v = sbt("s_v%d" % s, [128, 2, TKC, 128], BF16)
        s_e = sbt("s_e%d" % s, [128, TKC, TQ], BF16)
        s_den = sbt("s_den%d" % s, [1, TQ], F32)
        s_rec = sbt("s_rec%d" % s, [1, TQ], F32)
        s_bc = sbt("s_bc%d" % s, [128, TQ], F32)
        s_ob = sbt("s_ob%d" % s, [128, 2, TQ], BF16)

        wpT_r = self.wpT_s.rearrange("c p (hh v o) -> p c hh v o", hh=NH, v=3)
        t_xcast = self.t_cast_x[s]

        t_prev_phase = getattr(self, "t_y_done", 0) if s == 1 else 0

        @self.sec("sync")
        def _(sync, t_prev_phase=t_prev_phase):
            sync.wait_ge(dL.h, dL.n)
            sync.wait_ge(dG.h, t_xcast)
            if t_prev_phase > 0:
                sync.wait_ge(dG.h, t_prev_phase)
            for c in range(DC):
                sync.dma_start(out=s_xT[:, c, :],
                               in_=self.x_bf[s, :, c * 128:(c + 1) * 128],
                               transpose=True).then_inc(dL.h, 16)
        dL.n += 16 * DC
        t_xT = dL.n

        wp_rows = self.wp_bf.rearrange("h v r d -> (h v r) d")
        s_wph_f = s_wph.rearrange("p b c v o -> p b c (v o)")

        def load_wph(sync, h):
            buf = h % 2
            if s == 0 and h < 8:
                for c in range(DC):
                    sync.dma_start(out=s_wph_f[:, buf, c, :],
                                   in_=wp_rows[h * 384:(h + 1) * 384,
                                               c * 128:(c + 1) * 128],
                                   transpose=True).then_inc(dL.h, 16)
                return 16 * DC
            sync.dma_start(out=s_wph[:, buf],
                           in_=wpT_r[:, :, h]).then_inc(dL.h, 16)
            return 16

        t_wph = []

        n_h0 = [0]

        @self.sec("sync")
        def _(sync):
            sync.wait_ge(dL.h, dL.n)
            if not (s == 0 and 0 < 8):
                sync.wait_ge(dG.h, self.t_scratch["wp"])
            else:
                sync.wait_ge(dG.h, self.t_cast_wp)
            n_h0[0] = load_wph(sync, 0)
        dL.n += 16 * DC if s == 0 else 16
        t_wph.append(dL.n)

        q_evac_ve = [self.t_consts_v, self.t_consts_v]
        kv_evac_sc = [0, 0]
        attn_pe = [0, 0]
        qkv_pe = [0, 0]
        bank_free_sc = [0, 0]
        e_free_pe = 0
        oev_prev_ve = 0
        ob_st = [0, 0]
        o_evac_count = 0
        t_os_free = getattr(self, "t_wout_ols", 0)
        wd_pieces = []

        for h in range(NH):
            buf = h % 2
            t_w = t_wph[h]

            if h + 1 < NH:
                t_prev_qkv = qkv_pe[(h + 1) % 2]

                if s == 0:
                    wd_pieces.extend(self._prep_wd_chunk(h))
                    if h >= 3:
                        self._drain_passb(4)
                else:
                    if h >= 2:
                        self._drain_passb(6)

                nx = h + 1

                @self.sec("sync")
                def _(sync, h=h, nx=nx, t_prev_qkv=t_prev_qkv):
                    sync.wait_ge(dL.h, dL.n)
                    if not (s == 0 and nx < 8):
                        sync.wait_ge(dG.h, self.t_scratch["wp"])
                    if h >= 1:
                        sync.wait_ge(pe.h, t_prev_qkv)
                    load_wph(sync, nx)
                dL.n += 16 * DC if (s == 0 and nx < 8) else 16
                t_wph.append(dL.n)

            t_qe, t_ke, t_at = q_evac_ve[buf], kv_evac_sc[buf], attn_pe[buf]
            pe0, ve0, sc0 = pe.n, ve.n, sc.n

            @self.sec("tensor")
            def _(te, buf=buf, t_w=t_w, t_qe=t_qe, t_ke=t_ke, t_at=t_at,
                  ve0=ve0, sc0=sc0):
                te.wait_ge(dL.h, max(t_w, t_xT))
                te.wait_ge(pe.h, t_at)
                te.wait_ge(ve.h, t_qe)
                te.wait_ge(sc.h, t_ke)
                for jt in range(NTQ):
                    if jt > 0:
                        te.wait_ge(ve.h, ve0 + jt)
                        te.wait_ge(sc.h, sc0 + 2 * jt)
                    tq = slice(jt * TQ, (jt + 1) * TQ)
                    for c in range(DC):
                        te.matmul(ps[:, 4, :], s_wph[:, buf, c, 0, :], s_xT[:, c, tq],
                                  start=(c == 0), stop=(c == DC - 1))
                        te.matmul(ps[:, 5, :], s_wph[:, buf, c, 1, :], s_xT[:, c, tq],
                                  start=(c == 0), stop=(c == DC - 1))
                        ins = te.matmul(ps[:, 6, :], s_wph[:, buf, c, 2, :],
                                        s_xT[:, c, tq],
                                        start=(c == 0), stop=(c == DC - 1))
                    ins.then_inc(pe.h, 1)
            pe.n += NTQ

            @self.sec("vector")
            def _(v, buf=buf, pe0=pe0):
                for jt in range(NTQ):
                    tq = slice(jt * TQ, (jt + 1) * TQ)
                    v.wait_ge(pe.h, pe0 + jt + 1)
                    v.tensor_scalar(s_q[:, buf, tq], ps[:, 4, :], RS128, None,
                                    ALU.mult)
            q_evac_ve[buf] = ve.n

            @self.sec("scalar")
            def _(scal, buf=buf, pe0=pe0):
                for jt in range(NTQ):
                    tq = slice(jt * TQ, (jt + 1) * TQ)
                    scal.wait_ge(pe.h, pe0 + jt + 1)
                    scal.copy(s_k[:, buf, tq], ps[:, 5, :])
                    scal.copy(s_vT[:, tq], ps[:, 6, :])
            t_v_evac = sc.n
            qkv_pe[buf] = pe.n

            pe1, sc1 = pe.n, sc.n

            @self.sec("tensor")
            def _(te, t_v_evac=t_v_evac, sc1=sc1):
                te.wait_ge(sc.h, t_v_evac)
                for g in range(4):
                    if g > 0:
                        te.wait_ge(sc.h, sc1 + g)
                    for k in range(4):
                        cc = 4 * g + k
                        ins = te.transpose(ptr[:, k * 128:(k + 1) * 128],
                                           s_vT[:, cc * 128:(cc + 1) * 128],
                                           self.c_id[:])
                    ins.then_inc(pe.h, 1)
            pe.n += 4

            @self.sec("scalar")
            def _(scal, buf=buf, pe1=pe1):
                for g in range(4):
                    scal.wait_ge(pe.h, pe1 + g + 1)
                    scal.copy(s_v[:, buf, 4 * g:4 * g + 4, :], ptr[:])
            kv_evac_sc[buf] = sc.n

            for jt in range(NTQ):
                chunks = list(range(4 * jt, 4 * jt + 4)) + list(range(0, 4 * jt))
                n = len(chunks)
                tq = slice(jt * TQ, (jt + 1) * TQ)

                s_base = pe.n
                e_base = sc.n
                m_base = ve.n
                t_efree = e_free_pe
                t_oprev = oev_prev_ve
                bfree = list(bank_free_sc)

                @self.sec("tensor")
                def _(te, buf=buf, chunks=chunks, n=n, tq=tq, s_base=s_base,
                      e_base=e_base, m_base=m_base, t_oprev=t_oprev, bfree=bfree):
                    LAG = 3

                    def s_mm(i):
                        c = chunks[i]
                        if i >= 2:
                            te.wait_ge(sc.h, e_base + i - 1)
                        elif bfree[i % 2] > 0:
                            te.wait_ge(sc.h, bfree[i % 2])
                        te.matmul(ps[:, i % 2, :],
                                  s_k[:, buf, c * 128:(c + 1) * 128],
                                  s_q[:, buf, tq], start=True,
                                  stop=True).then_inc(pe.h, 1)

                    def acc_mm(i):
                        c = chunks[i]
                        first = (i == 0)
                        last = (i == n - 1)
                        if i < 4:
                            te.wait_ge(ve.h, m_base + i + 1)
                        else:
                            te.wait_ge(sc.h, e_base + i + 1)
                        if first and t_oprev > 0:
                            te.wait_ge(ve.h, t_oprev)
                        te.matmul(ps[0:1, 3, :], self.c_ones_b[:], s_e[:, c, :],
                                  start=first, stop=last)
                        ins = te.matmul(ps[:, 2, :], s_v[:, buf, c, :], s_e[:, c, :],
                                        start=first, stop=last)
                        if last:
                            ins.then_inc(pe.h, 1)

                    for i in range(n):
                        s_mm(i)
                        if i >= LAG:
                            acc_mm(i - LAG)
                    for i in range(max(0, n - LAG), n):
                        acc_mm(i)
                pe.n += n + 1
                t_acc = pe.n

                @self.sec("scalar")
                def _(scal, chunks=chunks, s_base=s_base, t_efree=t_efree):
                    for i, c in enumerate(chunks):
                        scal.wait_ge(pe.h, s_base + i + 1)
                        if i == 0 and t_efree > 0:
                            scal.wait_ge(pe.h, t_efree)
                        scal.activation(s_e[:, c, :], ps[:, i % 2, :],
                                        AF.Exp)
                bank_free_sc = [e_base + n - 1, e_base + n]

                @self.sec("vector")
                def _(v, chunks=chunks, e_base=e_base):
                    for i in range(4):
                        c = chunks[i]
                        v.wait_ge(sc.h, e_base + i + 1)
                        v.tensor_tensor(s_e[:, c, :], s_e[:, c, :],
                                        self.c_mask[:, i, :], ALU.mult)

                @self.sec("vector")
                def _(v, t_acc=t_acc):
                    v.wait_ge(pe.h, t_acc)
                    v.tensor_copy(s_den[:], ps[0:1, 3, :])
                    v.reciprocal(s_rec[:], s_den[:])
                t_rec = ve.n

                @self.sec("tensor")
                def _(te, t_rec=t_rec):
                    te.wait_ge(ve.h, t_rec)
                    te.matmul(ps[:, 3, :], self.c_ones_f[:], s_rec[:],
                              start=True, stop=True).then_inc(pe.h, 1)
                pe.n += 1
                t_bcmm = pe.n

                obuf = o_evac_count % 2
                t_ob_free = ob_st[obuf]

                @self.sec("vector")
                def _(v, obuf=obuf, t_bcmm=t_bcmm, t_ob_free=t_ob_free):
                    v.wait_ge(pe.h, t_bcmm)
                    v.tensor_copy(s_bc[:], ps[:, 3, :])
                    if t_ob_free > 0:
                        v.wait_ge(dG.h, t_ob_free)
                    v.tensor_tensor(s_ob[:, obuf, :], ps[:, 2, :], s_bc[:],
                                    ALU.mult)
                oev_prev_ve = ve.n
                t_oe = ve.n
                e_free_pe = t_acc

                if wd_pieces:
                    wd_pieces.pop(0)()
                    if len(wd_pieces) > 12:
                        wd_pieces.pop(0)()

                @self.sec("gpsimd")
                def _(g, h=h, jt=jt, obuf=obuf, t_oe=t_oe, t_os_free=t_os_free):
                    g.wait_ge(dG.h, dG.n)
                    g.wait_ge(ve.h, t_oe)
                    if t_os_free > 0:
                        g.wait_ge(dL.h, t_os_free)
                    g.dma_start(out=self.o_s[h * 128:(h + 1) * 128,
                                             jt * TQ:(jt + 1) * TQ],
                                in_=s_ob[:, obuf, :]).then_inc(dG.h, 16)
                dG.n += 16
                ob_st[obuf] = dG.n
                o_evac_count += 1
                t_os_free = 0

            attn_pe[buf] = pe.n

        if s == 0:
            wd_pieces.extend(self._prep_wd_chunk(DC - 1))
        while wd_pieces:
            wd_pieces.pop(0)()
        self.t_attn_o_st = dG.n
        if s == 0 and "os" in self.taps:
            t_d = dG.n

            @self.sec("gpsimd")
            def _(g, t_d=t_d):
                g.wait_ge(dG.h, t_d)
                g.dma_start(out=self.taps["os"], in_=self.o_s).then_inc(dG.h, 16)
            dG.n += 16

    def _wout_seq(self, s, sbt):
        dL, dG, pe, sc, ve = self.dL, self.dG, self.pe, self.sc, self.ve
        ps = self.ps
        NT = SEQ // 128

        s_woT = sbt("s_woT%d" % s, [128, DC, D], BF16)
        s_ol = sbt("s_ol%d" % s, [128, 2, DC, 128], BF16)
        s_xr = sbt("s_xr%d" % s, [128, 2, D], F32)
        s_r = sbt("s_r%d" % s, [128, D], F32)
        s_y = sbt("s_y%d" % s, [128, 2, D], F32)
        s_yb = sbt("s_yb%d" % s, [128, 2, D], BF16)
        s_ln = sbt("s_ln%d" % s, [128, 56], F32)

        t_o_st = self.t_attn_o_st
        o_sr = self.o_s.rearrange("(c p) t -> p c t", p=128)

        @self.sec("sync")
        def _(sync):
            sync.wait_ge(dL.h, dL.n)
            sync.wait_ge(dG.h, self.t_scratch["wo"])
            sync.wait_ge(dG.h, t_o_st)
            sync.dma_start(out=s_woT[:],
                           in_=self.woT_s.rearrange("c p d -> p c d")).then_inc(dL.h, 16)
        dL.n += 16
        t_woT = dL.n

        def load_ol(sync, tc):
            sync.dma_start(out=s_ol[:, tc % 2, :, :],
                           in_=o_sr[:, :, tc * 128:(tc + 1) * 128]).then_inc(dL.h, 16)

        def load_xr(sync, tc):
            sync.dma_start(out=s_xr[:, tc % 2, :],
                           in_=self.x[s, tc * 128:(tc + 1) * 128, :]).then_inc(dL.h, 16)

        @self.sec("sync")
        def _(sync):
            sync.wait_ge(dL.h, dL.n)
            sync.wait_ge(dG.h, t_o_st)
            load_ol(sync, 0)
            load_xr(sync, 0)
        dL.n += 32
        t_ol = [dL.n]

        bank_ve = [0] * 6
        ol_read_pe = [0, 0]
        y_st = [0, 0]
        ln_done_ve = [0, 0]
        for tc in range(NT):
            buf = tc % 2
            if tc + 1 < NT:
                t_free = ol_read_pe[(tc + 1) % 2]
                t_yfree = ln_done_ve[(tc + 1) % 2]

                @self.sec("sync")
                def _(sync, tc=tc, t_free=t_free, t_yfree=t_yfree):
                    sync.wait_ge(dL.h, dL.n)
                    if t_free > 0:
                        sync.wait_ge(pe.h, t_free)
                    if t_yfree > 0:
                        sync.wait_ge(ve.h, t_yfree)
                    load_ol(sync, tc + 1)
                    load_xr(sync, tc + 1)
                dL.n += 32
                t_ol.append(dL.n)

            pe0 = pe.n
            t_this = t_ol[tc]
            bve = list(bank_ve)

            @self.sec("tensor")
            def _(te, tc=tc, buf=buf, t_this=t_this, bve=bve):
                te.wait_ge(dL.h, max(t_this, t_woT))
                for dt in range(4):
                    bank = (tc * 4 + dt) % 6
                    if bve[bank] > 0:
                        te.wait_ge(ve.h, bve[bank])
                    for c in range(DC):
                        ins = te.matmul(ps[:, bank, :], s_ol[:, buf, c, :],
                                        s_woT[:, c, dt * 512:(dt + 1) * 512],
                                        start=(c == 0), stop=(c == DC - 1))
                    ins.then_inc(pe.h, 1)
            pe.n += 4
            ol_read_pe[buf] = pe.n

            ve0 = ve.n

            @self.sec("vector")
            def _(v, tc=tc, buf=buf, pe0=pe0):
                for dt in range(4):
                    bank = (tc * 4 + dt) % 6
                    v.wait_ge(pe.h, pe0 + dt + 1)
                    v.tensor_tensor(s_r[:, dt * 512:(dt + 1) * 512], ps[:, bank, :],
                                    s_xr[:, buf, dt * 512:(dt + 1) * 512],
                                    ALU.add)
                for g in range(8):
                    v.bn_stats(s_ln[:, 6 * g:6 * g + 6],
                               s_r[:, 256 * g:256 * (g + 1)])
                v.bn_aggr(s_ln[:, 48:50], s_ln[:, 0:48])
            for dt in range(4):
                bank_ve[(tc * 4 + dt) % 6] = ve0 + dt + 1
            t_var = ve.n

            @self.sec("scalar")
            def _(scal, t_var=t_var):
                scal.wait_ge(ve.h, t_var)
                scal.activation(s_ln[:, 50:51], s_ln[:, 49:50], AF.Sqrt,
                                bias=self.c_eps[:])
            t_std = sc.n
            t_yst = y_st[buf]

            @self.sec("vector")
            def _(v, buf=buf, t_std=t_std, t_yst=t_yst):
                v.wait_ge(sc.h, t_std)
                v.reciprocal(s_ln[:, 51:52], s_ln[:, 50:51])
                if t_yst > 0:
                    v.wait_ge(dG.h, t_yst)
                v.tensor_scalar(s_y[:, buf, :], s_r[:], s_ln[:, 48:49], s_ln[:, 51:52],
                                ALU.subtract, ALU.mult)
            ln_done_ve[buf] = ve.n
            t_ln = ve.n

            @self.sec("scalar")
            def _(scal, buf=buf, t_ln=t_ln):
                scal.wait_ge(ve.h, t_ln)
                scal.copy(s_yb[:, buf, :], s_y[:, buf, :])
            t_yb = sc.n

            @self.sec("gpsimd")
            def _(g, tc=tc, buf=buf, t_ln=t_ln, t_yb=t_yb):
                row = s * SEQ + tc * 128
                g.wait_ge(dG.h, dG.n)
                g.wait_ge(ve.h, t_ln)
                g.dma_start(out=self.y_s[row:row + 128, :],
                            in_=s_y[:, buf, :]).then_inc(dG.h, 16)
                g.wait_ge(sc.h, t_yb)
                g.dma_start(out=self.y_bf[row:row + 128, :],
                            in_=s_yb[:, buf, :]).then_inc(dG.h, 16)
            dG.n += 32
            y_st[buf] = dG.n

        self.t_wout_ols = t_ol[-1]
        self.t_y_done = dG.n
        if s == 1 and "y" in self.taps:
            t_d = dG.n

            @self.sec("gpsimd")
            def _(g, t_d=t_d):
                g.wait_ge(dG.h, t_d)
                g.dma_start(out=self.taps["y"], in_=self.y_s).then_inc(dG.h, 16)
            dG.n += 16

    def _ffn(self, sbt):
        dL, dG, dS, pe, sc, ve = self.dL, self.dG, self.dS, self.pe, self.sc, self.ve
        ps = self.ps
        H = 512
        NT = FT // 128

        s_hsu = sbt("s_hsu", [128, DC, 128], BF16)
        s_u = sbt("s_u", [128, DC, FT], BF16)
        s_m1 = sbt("s_m1", [128, DC, FT], F32)
        s_zz = sbt("s_zz", [128, 9, FT], BF16)
        s_wu = sbt("s_wu", [128, DC, IB], BF16)
        s_wg = sbt("s_wg", [128, DC, IB], BF16)
        s_wd = sbt("s_wd", [128, IBC, D], BF16)
        s_yr = sbt("s_yr", [128, D], F32)
        s_r = sbt("s_rf", [128, D], F32)
        s_o = sbt("s_of", [128, D], F32)
        s_ln = sbt("s_lnf", [128, 56], F32)

        z_of = lambda r: s_zz[:, r, :]
        z2_of = lambda r: s_zz[:, 4 + r, :]
        sg_view = s_zz[:, 8, :]
        acc = s_m1.rearrange("p c f -> p (c f)")

        t_cv = self.t_consts_v

        @self.sec("vector")
        def _(v, t_cv=t_cv):
            v.wait_ge(ve.h, t_cv)
            for c in range(DC):
                v.tensor_scalar(s_hsu[:, c, :], self.c_h128[:],
                                self.v_suug[:, c:c + 1], None, ALU.mult)
        t_hsu = ve.n

        t_y = self.t_y_done
        u_free = 0
        epi_ve = 0
        wu_read, wg_read, wd_read = 0, 0, 0
        z2_sc_banks = [0, 0]
        dn_bank_ve = [0] * 4
        o_st_prev = 0
        yr_used_ve = 0

        for ft in range(NFT):
            row0 = ft * FT
            t_uf0 = u_free

            @self.sec("sync")
            def _(sync, ft=ft, row0=row0, t_uf0=t_uf0, t_gall=dG.n):
                sync.wait_ge(dL.h, dL.n)
                if ft == 0:
                    sync.wait_ge(dG.h, t_gall)
                if t_uf0 > 0:
                    sync.wait_ge(pe.h, t_uf0)
                for c in range(DC):
                    sync.dma_start(out=s_u[:, c, :],
                                   in_=self.y_bf[row0:row0 + FT, c * 128:(c + 1) * 128],
                                   transpose=True).then_inc(dL.h, 16)
            dL.n += 16 * DC
            t_yT = dL.n

            wupT_r = self.wupT_s.rearrange("c p i -> p c i")
            wgtT_r = self.wgtT_s.rearrange("c p i -> p c i")
            wd3T_r = self.wd3T_s.rearrange("c p d -> p c d")

            def load_wu(q, ib):
                q.dma_start(out=s_wu[:],
                            in_=wupT_r[:, :, ib * IB:(ib + 1) * IB]).then_inc(dL.h, 16)

            def load_wg(q, ib):
                q.dma_start(out=s_wg[:],
                            in_=wgtT_r[:, :, ib * IB:(ib + 1) * IB]).then_inc(dS.h, 16)

            def load_wd(q, ib):
                q.dma_start(out=s_wd[:],
                            in_=wd3T_r[:, ib * IBC:(ib + 1) * IBC, :]).then_inc(dL.h, 16)

            t_wua, t_wga, t_wda = wu_read, wg_read, wd_read

            @self.sec("sync")
            def _(sync, ft=ft, t_wua=t_wua, t_wda=t_wda):
                sync.wait_ge(dL.h, dL.n)
                if ft == 0:
                    sync.wait_ge(dG.h, self.t_scratch["wup"])
                    sync.wait_ge(dG.h, self.t_scratch["wd3"])
                if t_wua > 0:
                    sync.wait_ge(pe.h, t_wua)
                load_wu(sync, 0)
                sync.wait_ge(dL.h, dL.n + 16)
                if t_wda > 0:
                    sync.wait_ge(pe.h, t_wda)
                load_wd(sync, 0)
            dL.n += 32
            t_wu = [dL.n - 16]
            t_wd = [dL.n]

            @self.sec("scalar")
            def _(scal, ft=ft, t_wga=t_wga):
                scal.wait_ge(dS.h, dS.n)
                if ft == 0:
                    scal.wait_ge(dG.h, self.t_scratch["wgt"])
                    scal.wait_ge(dG.h, t_y)
                if t_wga > 0:
                    scal.wait_ge(pe.h, t_wga)
                load_wg(scal, 0)
            dS.n += 16
            t_wg = [dS.n]

            if DEBUG_FFN == "yt":
                self._dump(s_u.rearrange("p c f -> p (c f)"), DC * FT, 2)
                return

            pe0, sc0 = pe.n, sc.n
            epi_tc = list(getattr(self, "_epi_tc_read", [0] * (NT * NFT)))
            zb = list(z2_sc_banks)

            @self.sec("tensor")
            def _(te, t_yT=t_yT, t_hsu=t_hsu, sc0=sc0, zb=zb):
                te.wait_ge(dL.h, t_yT)
                te.wait_ge(ve.h, t_hsu)
                for idx in range(2 * DC):
                    hf, c = idx // DC, idx % DC
                    if idx >= 2:
                        te.wait_ge(sc.h, sc0 + idx - 1)
                    elif zb[idx % 2] > 0:
                        te.wait_ge(sc.h, zb[idx % 2])
                    te.matmul(ps[:, idx % 2, :], s_hsu[:, c, :],
                              s_u[:, c, hf * H:(hf + 1) * H],
                              start=True, stop=True).then_inc(pe.h, 1)
            pe.n += 2 * DC

            @self.sec("scalar")
            def _(scal, ft=ft, pe0=pe0, epi_tc=epi_tc):
                for idx in range(2 * DC):
                    hf, c = idx // DC, idx % DC
                    if ft > 0 and epi_tc[(ft - 1) * NT + c // 2] > 0:
                        scal.wait_ge(ve.h, epi_tc[(ft - 1) * NT + c // 2])
                    scal.wait_ge(pe.h, pe0 + idx + 1)
                    scal.copy(s_m1[:, c, hf * H:(hf + 1) * H],
                              ps[:, idx % 2, :])
            t_m1h = [sc0 + DC, sc0 + 2 * DC]

            if DEBUG_FFN == "m1":
                self._dump(s_m1.rearrange("p c f -> p (c f)"), DC * FT, 4)
                return

            t_u_half = []
            for hf in range(2):
                @self.sec("vector")
                def _(v, hf=hf, t_m1h=t_m1h):
                    v.wait_ge(sc.h, t_m1h[hf])
                    sl = slice(hf * H, (hf + 1) * H)
                    for stc in (8, 4, 2):
                        for g0 in range(0, DC, 2 * stc):
                            for k in range(stc):
                                a = s_m1[:, g0 + k, sl]
                                b = s_m1[:, g0 + stc + k, sl]
                                v.tensor_add(a, a, b)
                                v.scalar_tensor_tensor(b, b, -2.0, a,
                                                       ALU.mult, ALU.add)
                    for k in range(0, DC, 2):
                        a = s_m1[:, k, sl]
                        b = s_m1[:, k + 1, sl]
                        v.tensor_add(s_u[:, k, sl], a, b)
                        v.tensor_sub(s_u[:, k + 1, sl], a, b)
                t_u_half.append(ve.n)
            t_u = ve.n

            if DEBUG_FFN == "u":
                self._dump(s_u.rearrange("p c f -> p (c f)"), DC * FT, 2)
                return

            t_z_prev_ve = 0
            t_sg_prev_sc = 0
            for ib in range(NIB):
                t_wub, t_wgb, t_wdb = t_wu[ib], t_wg[ib], t_wd[ib]
                peu0 = pe.n
                dn_prev = list(dn_bank_ve)

                @self.sec("tensor")
                def _(te, ib=ib, t_wub=t_wub, t_wgb=t_wgb, t_u=t_u,
                      t_u_half=t_u_half,
                      t_z_prev_ve=t_z_prev_ve, t_sg_prev_sc=t_sg_prev_sc,
                      dn_prev=dn_prev, ve_z0=ve.n, sc_silu0=sc.n):
                    if ib == 0:
                        te.wait_ge(ve.h, t_u_half[0])
                    else:
                        te.wait_ge(ve.h, t_u)
                    te.wait_ge(dL.h, t_wub)
                    te.wait_ge(dS.h, t_wgb)
                    for r in range(IBC):
                        if r == 0:
                            if t_sg_prev_sc > 0:
                                te.wait_ge(sc.h, t_sg_prev_sc)
                            gate_ve = max(dn_prev + [t_z_prev_ve])
                            if gate_ve > 0:
                                te.wait_ge(ve.h, gate_ve)
                        else:
                            te.wait_ge(ve.h, ve_z0 + 2 * r)
                            te.wait_ge(sc.h, sc_silu0 + 2 * r)
                        for hf in range(2):
                            if ib == 0 and r == 0 and hf == 1:
                                te.wait_ge(ve.h, t_u_half[1])
                            for c in range(DC):
                                te.matmul(ps[:, 2 + hf, :],
                                          s_wu[:, c, r * 128:(r + 1) * 128],
                                          s_u[:, c, hf * H:(hf + 1) * H],
                                          start=(c == 0), stop=(c == DC - 1))
                                ins = te.matmul(ps[:, 4 + hf, :],
                                                s_wg[:, c, r * 128:(r + 1) * 128],
                                                s_u[:, c, hf * H:(hf + 1) * H],
                                                start=(c == 0), stop=(c == DC - 1))
                        ins.then_inc(pe.h, 1)
                pe.n += IBC
                peu1 = pe.n
                if ib == NIB - 1:
                    u_free = pe.n

                sc_silu_base = sc.n

                @self.sec("scalar")
                def _(scal, ib=ib, peu0=peu0):
                    for r in range(IBC):
                        cc = ib * IBC + r
                        scal.wait_ge(pe.h, peu0 + r + 1)
                        for hf in range(2):
                            scal.activation(sg_view[:, hf * H:(hf + 1) * H],
                                            ps[:, 4 + hf, :], AF.Silu,
                                            scale=self.v_sgt[:, cc:cc + 1])
                t_sg_prev_sc = sc.n

                ve_z_base = ve.n

                @self.sec("vector")
                def _(v, ib=ib, sc_silu_base=sc_silu_base):
                    for r in range(IBC):
                        cc = ib * IBC + r
                        for hf in range(2):
                            v.wait_ge(sc.h, sc_silu_base + 2 * r + hf + 1)
                            v.scalar_tensor_tensor(
                                z_of(r)[:, hf * H:(hf + 1) * H], ps[:, 2 + hf, :],
                                self.v_sup[:, cc:cc + 1],
                                sg_view[:, hf * H:(hf + 1) * H],
                                ALU.mult, ALU.mult)
                t_z_prev_ve = ve.n

                if DEBUG_FFN == "z" and ib == 0:
                    self._dump(s_zz.rearrange("p c f -> p (c f)")[:, 0:4 * FT], 4 * FT, 2)
                    return

                pez0 = pe.n
                scz0 = sc.n
                zb2 = list(z2_sc_banks)

                @self.sec("tensor")
                def _(te, ve_z_base=ve_z_base, zb2=zb2, scz0=scz0):
                    for idx in range(2 * IBC):
                        r, hf = idx // 2, idx % 2
                        te.wait_ge(ve.h, ve_z_base + 2 * r + hf + 1)
                        if idx >= 2:
                            te.wait_ge(sc.h, scz0 + idx - 1)
                        elif zb2[idx % 2] > 0:
                            te.wait_ge(sc.h, zb2[idx % 2])
                        te.matmul(ps[:, idx % 2, :], self.c_h128[:],
                                  z_of(r)[:, hf * H:(hf + 1) * H],
                                  start=True, stop=True).then_inc(pe.h, 1)
                pe.n += 2 * IBC

                @self.sec("scalar")
                def _(scal, pez0=pez0):
                    for idx in range(2 * IBC):
                        r, hf = idx // 2, idx % 2
                        scal.wait_ge(pe.h, pez0 + idx + 1)
                        scal.copy(z2_of(r)[:, hf * H:(hf + 1) * H],
                                  ps[:, idx % 2, :])
                z2_sc_banks = [sc.n - 1, sc.n]
                t_z2_all = sc.n

                if DEBUG_FFN == "z2" and ib == 0:
                    self._dump(s_zz.rearrange("p c f -> p (c f)")[:, 4 * FT:8 * FT], 4 * FT, 2)
                    return

                ped0 = pe.n
                dn_prev2 = list(dn_bank_ve)

                ved0 = ve.n

                @self.sec("tensor")
                def _(te, t_wdb=t_wdb, t_z2=t_z2_all, dn_prev2=dn_prev2, ved0=ved0):
                    te.wait_ge(sc.h, t_z2)
                    te.wait_ge(dL.h, t_wdb)
                    for k in range(NT * 4):
                        tc, dt = k // 4, k % 4
                        bank = 2 + (k % 4)
                        if k < 4:
                            if dn_prev2[k % 4] > 0:
                                te.wait_ge(ve.h, dn_prev2[k % 4])
                        else:
                            te.wait_ge(ve.h, ved0 + k - 3)
                        for r in range(IBC):
                            ins = te.matmul(ps[:, bank, :],
                                            z2_of(r)[:, tc * 128:(tc + 1) * 128],
                                            s_wd[:, r, dt * 512:(dt + 1) * 512],
                                            start=(r == 0), stop=(r == IBC - 1))
                        ins.then_inc(pe.h, 1)
                pe.n += NT * 4
                ved0 = ve.n

                @self.sec("vector")
                def _(v, ib=ib, ped0=ped0):
                    for k in range(NT * 4):
                        tc, dt = k // 4, k % 4
                        bank = 2 + (k % 4)
                        v.wait_ge(pe.h, ped0 + k + 1)
                        dst = acc[:, tc * D + dt * 512: tc * D + (dt + 1) * 512]
                        if ib == 0:
                            v.tensor_copy(dst, ps[:, bank, :])
                        else:
                            v.tensor_add(dst, dst, ps[:, bank, :])
                for k in range(NT * 4):
                    dn_bank_ve[k % 4] = ved0 + k + 1

                if ib + 1 < NIB:
                    t_um = peu1
                    ped_end = pe.n

                    @self.sec("sync")
                    def _(sync, ib=ib, t_um=t_um, ped_end=ped_end):
                        sync.wait_ge(dL.h, dL.n)
                        sync.wait_ge(pe.h, t_um)
                        load_wu(sync, ib + 1)
                        sync.wait_ge(dL.h, dL.n + 16)
                        sync.wait_ge(pe.h, ped_end)
                        load_wd(sync, ib + 1)
                    dL.n += 32
                    t_wu.append(dL.n - 16)
                    t_wd.append(dL.n)

                    @self.sec("scalar")
                    def _(scal, ib=ib, t_um=t_um):
                        scal.wait_ge(dS.h, dS.n)
                        scal.wait_ge(pe.h, t_um)
                        load_wg(scal, ib + 1)
                    dS.n += 16
                    t_wg.append(dS.n)
            wu_read, wg_read, wd_read = u_free, u_free, pe.n
            t_dn_all = ve.n

            if DEBUG_FFN == "acc":
                self._dump(acc[:, 0:DC * FT], DC * FT, 4)
                return

            for tc in range(NT):
                t_yru = yr_used_ve

                @self.sec("sync")
                def _(sync, row0=row0, tc=tc, t_yru=t_yru):
                    sync.wait_ge(dL.h, dL.n)
                    if t_yru > 0:
                        sync.wait_ge(ve.h, t_yru)
                    sync.dma_start(out=s_yr[:],
                                   in_=self.y_s[row0 + tc * 128:
                                                row0 + (tc + 1) * 128, :]).then_inc(dL.h, 16)
                dL.n += 16
                t_this = dL.n
                t_osp = o_st_prev
                veb0 = ve.n
                if not hasattr(self, "_epi_tc_read"):
                    self._epi_tc_read = [0] * (NT * NFT)
                self._epi_tc_read[ft * NT + tc] = veb0 + 1

                @self.sec("vector")
                def _(v, tc=tc, t_this=t_this, t_dn_all=t_dn_all):
                    v.wait_ge(dL.h, t_this)
                    v.wait_ge(ve.h, t_dn_all)
                    v.tensor_tensor(s_r[:], acc[:, tc * D:(tc + 1) * D],
                                    s_yr[:], ALU.add)
                    for g in range(8):
                        v.bn_stats(s_ln[:, 6 * g:6 * g + 6],
                                   s_r[:, 256 * g:256 * (g + 1)])
                    v.bn_aggr(s_ln[:, 48:50], s_ln[:, 0:48])
                yr_used_ve = veb0 + 1
                t_var = ve.n

                @self.sec("scalar")
                def _(scal, t_var=t_var):
                    scal.wait_ge(ve.h, t_var)
                    scal.activation(s_ln[:, 50:51], s_ln[:, 49:50], AF.Sqrt,
                                    bias=self.c_eps[:])
                t_std = sc.n

                @self.sec("vector")
                def _(v, t_std=t_std, t_osp=t_osp):
                    v.wait_ge(sc.h, t_std)
                    v.reciprocal(s_ln[:, 51:52], s_ln[:, 50:51])
                    if t_osp > 0:
                        v.wait_ge(dG.h, t_osp)
                    v.tensor_scalar(s_o[:], s_r[:], s_ln[:, 48:49], s_ln[:, 51:52],
                                    ALU.subtract, ALU.mult)
                t_ln = ve.n

                @self.sec("gpsimd")
                def _(g, row0=row0, tc=tc, t_ln=t_ln):
                    row = row0 + tc * 128
                    sq = row // SEQ
                    g.wait_ge(dG.h, dG.n)
                    g.wait_ge(ve.h, t_ln)
                    g.dma_start(out=self.out[sq, row % SEQ:row % SEQ + 128, :],
                                in_=s_o[:]).then_inc(dG.h, 16)
                dG.n += 16
                o_st_prev = dG.n
                epi_ve = ve.n

    def _dump(self, tile_ap, nelem, dt_bytes):
        dG, ve, sc, pe = self.dG, self.ve, self.sc, self.pe
        flat = self.out.rearrange("s t d -> (s t d)")
        dst = flat[0:128 * nelem].rearrange("(p f) -> p f", p=128)

        dL, dS = self.dL, self.dS

        @self.sec("gpsimd")
        def _(g):
            g.wait_ge(dG.h, dG.n)
            g.wait_ge(dL.h, dL.n)
            g.wait_ge(dS.h, dS.n)
            g.wait_ge(ve.h, ve.n)
            g.wait_ge(sc.h, sc.n)
            g.wait_ge(pe.h, pe.n)
            g.dma_start(out=dst, in_=tile_ap).then_inc(dG.h, 16)
        dG.n += 16


    def _finish(self):
        dL, dG, dS = self.dL, self.dG, self.dS
        if STOP_AFTER == "wout":
            t_y = self.t_y_done

            @self.sec("gpsimd")
            def _(g, t_y=t_y):
                g.wait_ge(dG.h, dG.n)
                g.wait_ge(dG.h, t_y)
                g.dma_start(out=self.out.rearrange("s t d -> (s t) d"),
                            in_=self.y_s).then_inc(dG.h, 16)
            dG.n += 16

        @self.sec("sync")
        def _(sync):
            sync.wait_ge(dG.h, dG.n)
            sync.wait_ge(dL.h, dL.n)
            sync.wait_ge(dS.h, dS.n)


_PROG_CACHE = {}


def _get_program():
    key = (STOP_AFTER, tuple(DEBUG_TAPS))
    if key not in _PROG_CACHE:
        b = Build()
        nc = b.build()
        _PROG_CACHE[key] = (nc, b)
    return _PROG_CACHE[key]


def kernel(L0_attn_x, L0_attn_w_project, L0_attn_w_out, L0_ffn_SU_ug,
           L0_ffn_SU_dp, L0_ffn_Wscale_ug, L0_ffn_Wscale_dp, L0_ffn_W_up,
           L0_ffn_W_gate, L0_ffn_W_down):
    nc, _b = _get_program()

    h128 = hadamard(128).astype(bfdt)
    ident = np.eye(128, dtype=np.float32).astype(bfdt)
    p = np.arange(128)[:, None]
    j = np.arange(TQ)[None, :]
    masks = np.stack([(j >= (r * 128 + p)).astype(np.float32) for r in range(NTQ)],
                     axis=1).astype(bfdt)

    x = np.ascontiguousarray(np.asarray(L0_attn_x, dtype=np.float32))
    com = {
        "wp": np.ascontiguousarray(np.asarray(L0_attn_w_project, dtype=np.float32)),
        "wo": np.ascontiguousarray(np.asarray(L0_attn_w_out, dtype=np.float32)),
        "wup": np.ascontiguousarray(np.asarray(L0_ffn_W_up, dtype=np.float32)),
        "wgt": np.ascontiguousarray(np.asarray(L0_ffn_W_gate, dtype=np.float32)),
        "wdn": np.ascontiguousarray(np.asarray(L0_ffn_W_down, dtype=np.float32)),
        "su_ug": np.ascontiguousarray(np.asarray(L0_ffn_SU_ug, dtype=np.float32).reshape(DC, 128).T),
        "su_dp": np.ascontiguousarray(np.asarray(L0_ffn_SU_dp, dtype=np.float32).reshape(IC, 128).T),
        "ws_ug": np.ascontiguousarray(np.asarray(L0_ffn_Wscale_ug, dtype=np.float32).reshape(2 * IC, 128).T),
        "ws_dp": np.ascontiguousarray(np.asarray(L0_ffn_Wscale_dp, dtype=np.float32).reshape(DC, 128).T),
        "h128": h128, "ident": ident, "masks": masks,
    }
    in_maps = []
    for core in range(NCORES):
        m = dict(com)
        m["x"] = np.ascontiguousarray(x[2 * core:2 * core + 2])
        in_maps.append(m)

    res = run_bass_kernel_spmd(nc, in_maps, core_ids=list(range(NCORES)))
    out = np.concatenate([np.asarray(r["out"], dtype=np.float32)
                          for r in res.results], axis=0)
    kernel._last_results = res.results
    return out.reshape(16, SEQ, D)


# version 14
# speedup vs baseline: 1.1727x; 1.1727x over previous
import math
from contextlib import ExitStack

import numpy as np
import ml_dtypes

import concourse.bass as bass
import concourse.mybir as mybir
from concourse.bass_utils import run_bass_kernel_spmd

F32 = mybir.dt.float32
BF16 = mybir.dt.bfloat16
AF = mybir.ActivationFunctionType
ALU = mybir.AluOpType
AX = mybir.AxisListType
bfdt = ml_dtypes.bfloat16

NCORES = 8
SEQ = 2048
D = 2048
DC = D // 128
NH = 16
I = 8192
IC = I // 128
TQ = 512
NTQ = SEQ // TQ
TKC = SEQ // 128
FT = 1024
NFT = 2 * SEQ // FT
IB = 512
NIB = I // IB
IBC = IB // 128
RS128 = 1.0 / math.sqrt(128.0)
LNEPS = 1e-5

DEBUG_TAPS = ()
STOP_AFTER = None
DEBUG_FFN = None


def hadamard(n):
    idx = np.arange(n)
    a = idx[:, None] & idx[None, :]
    pc = np.zeros((n, n), dtype=np.int64)
    v = a.copy()
    while v.any():
        pc += v & 1
        v >>= 1
    return np.where(pc % 2 == 0, 1.0, -1.0).astype(np.float32)


class Sem:
    def __init__(self, handle):
        self.h = handle
        self.n = 0


CHAIN_PASS = {"wait_ge", "dma_start", "nop", "engine_nop", "dma_start_transpose"}


class Chained:

    def __init__(self, raw, sem):
        self._raw = raw
        self._sem = sem

    def wait_ge(self, h, v):
        return self._raw.wait_ge(h, v)

    def dma_start(self, *a, **k):
        return self._raw.dma_start(*a, **k)

    def __getattr__(self, name):
        f = getattr(self._raw, name)
        if name.startswith("_") or not callable(f) or name in CHAIN_PASS:
            return f
        sem = self._sem
        raw = self._raw

        def wrapped(*a, **k):
            raw.wait_ge(sem.h, sem.n)
            ins = f(*a, **k)
            ins.then_inc(sem.h, 1)
            sem.n += 1
            return ins

        return wrapped


class Build:
    def __init__(self):
        self.nc = bass.Bass("TRN2", target_bir_lowering=False,
                    detect_race_conditions=True)
        self.taps = {}

    def sec(self, engine_name):
        deco = getattr(self.block, engine_name)
        if engine_name == "vector":
            return lambda fn: deco(lambda eng: fn(Chained(eng, self.ve)))
        if engine_name == "scalar":
            return lambda fn: deco(lambda eng: fn(Chained(eng, self.sc)))
        if engine_name == "gpsimd":
            return lambda fn: deco(lambda eng: fn(Chained(eng, self.gp)))
        return deco

    def build(self):
        nc = self.nc
        ein, eout = "ExternalInput", "ExternalOutput"
        self.x = nc.dram_tensor("x", [2, SEQ, D], F32, kind=ein).ap()
        self.wp = nc.dram_tensor("wp", [3 * D, D], F32, kind=ein).ap()
        self.wo = nc.dram_tensor("wo", [D, D], F32, kind=ein).ap()
        self.wup = nc.dram_tensor("wup", [I, D], F32, kind=ein).ap()
        self.wgt = nc.dram_tensor("wgt", [I, D], F32, kind=ein).ap()
        self.wdn = nc.dram_tensor("wdn", [D, I], F32, kind=ein).ap()
        self.su_ug = nc.dram_tensor("su_ug", [128, DC], F32, kind=ein).ap()
        self.su_dp = nc.dram_tensor("su_dp", [128, IC], F32, kind=ein).ap()
        self.ws_ug = nc.dram_tensor("ws_ug", [128, 2 * IC], F32, kind=ein).ap()
        self.ws_dp = nc.dram_tensor("ws_dp", [128, DC], F32, kind=ein).ap()
        self.h128 = nc.dram_tensor("h128", [128, 128], BF16, kind=ein).ap()
        self.ident = nc.dram_tensor("ident", [128, 128], BF16, kind=ein).ap()
        self.masks = nc.dram_tensor("masks", [128, NTQ, TQ], BF16, kind=ein).ap()
        self.out = nc.dram_tensor("out", [2, SEQ, D], F32, kind=eout).ap()

        dram = lambda name, shape, dt: nc.dram_tensor(name, shape, dt, kind="Internal").ap()
        self.x_bf = dram("x_bf", [2, SEQ, D], BF16)
        self.wp_bf = dram("wp_bf", [NH, 3, 128, D], BF16)
        self.wo_bf = dram("wo_bf", [D, D], BF16)
        self.wup_bf = dram("wup_bf", [I, D], BF16)
        self.wgt_bf = dram("wgt_bf", [I, D], BF16)
        self.wd3_bf = dram("wd3_bf", [D, I], BF16)
        self.wpT_s = dram("wpT_s", [DC, 128, 3 * D], BF16)
        self.woT_s = dram("woT_s", [DC, 128, D], BF16)
        self.wupT_s = dram("wupT_s", [DC, 128, I], BF16)
        self.wgtT_s = dram("wgtT_s", [DC, 128, I], BF16)
        self.wd3T_s = dram("wd3T_s", [IC, 128, D], BF16)
        self.o_s = dram("o_s", [D, SEQ], BF16)
        self.y_s = dram("y_s", [2 * SEQ, D], F32)
        self.y_bf = dram("y_bf", [2 * SEQ, D], BF16)

        for t in DEBUG_TAPS:
            shape = {"os": [D, SEQ], "y": [2 * SEQ, D]}[t]
            self.taps[t] = nc.dram_tensor("tap_" + t, shape, F32, kind=eout).ap()

        with ExitStack() as ctx:
            en = ctx.enter_context
            sbt = lambda name, shape, dt: en(nc.sbuf_tensor(name, shape, dt))
            self.dL = Sem(en(nc.semaphore("dL")))
            self.dG = Sem(en(nc.semaphore("dG")))
            self.dS = Sem(en(nc.semaphore("dS")))
            self.pe = Sem(en(nc.semaphore("pe")))
            self.gp = Sem(en(nc.semaphore("gp")))
            self.sc = Sem(en(nc.semaphore("sc")))
            self.ve = Sem(en(nc.semaphore("ve")))

            self.c_h128 = sbt("c_h128", [128, 128], BF16)
            self.c_id = sbt("c_id", [128, 128], BF16)
            self.c_mask = sbt("c_mask", [128, NTQ, TQ], BF16)
            self.c_ones_b = sbt("c_ones_b", [128, 1], BF16)
            self.c_ones_f = sbt("c_ones_f", [1, 128], F32)
            self.c_eps = sbt("c_eps", [128, 1], F32)
            self.v_sup = sbt("v_sup", [128, IC], F32)
            self.v_sgt = sbt("v_sgt", [128, IC], F32)
            self.v_sdn = sbt("v_sdn", [128, DC], F32)
            self.v_suug = sbt("v_suug", [128, DC], F32)
            self.v_tmp = sbt("v_tmp", [128, IC], F32)

            self.ps = en(nc.psum_tensor("ps", [128, 7, 512], F32))
            self.ptr = en(nc.psum_tensor("ptr", [128, 512], BF16))

            self.block = en(nc.Block())
            self._consts()
            with ExitStack() as pctx:
                psb = lambda n_, s_, d_: pctx.enter_context(nc.sbuf_tensor(n_, s_, d_))
                self.p_wa = psb("p_wa", [128, I // 2], F32)
                self.p_wb = psb("p_wb", [128, I // 2], F32)
                self.p_wo = psb("p_wo", [128, I // 2], BF16)
                self.p_land = psb("p_land", [128, I], BF16)
                self._prep()
                self._prep_passb_build()
                for s in range(2):
                    with ExitStack() as actx:
                        asb = lambda n_, s_, d_: actx.enter_context(
                            nc.sbuf_tensor(n_, s_, d_))
                        self._attn_seq(s, asb)
                    with ExitStack() as wctx:
                        wsb = lambda n_, s_, d_: wctx.enter_context(
                            nc.sbuf_tensor(n_, s_, d_))
                        self._wout_seq(s, wsb)
            self._drain_passb(10 ** 9)
            if STOP_AFTER is None:
                with ExitStack() as fctx:
                    fsb = lambda n_, s_, d_: fctx.enter_context(
                        nc.sbuf_tensor(n_, s_, d_))
                    self._ffn(fsb)
            self._finish()
        return nc

    def _consts(self):
        dL, ve = self.dL, self.ve

        @self.sec("sync")
        def _(sync):
            sync.wait_ge(dL.h, dL.n)
            sync.dma_start(out=self.c_h128[:], in_=self.h128).then_inc(dL.h, 16)
            sync.dma_start(out=self.c_id[:], in_=self.ident).then_inc(dL.h, 16)
            sync.dma_start(out=self.c_mask[:], in_=self.masks).then_inc(dL.h, 16)
            sync.dma_start(out=self.v_suug[:], in_=self.su_ug).then_inc(dL.h, 16)
            sync.dma_start(out=self.v_sup[:], in_=self.su_dp).then_inc(dL.h, 16)
            sync.dma_start(out=self.v_sgt[:], in_=self.ws_ug[:, IC:]).then_inc(dL.h, 16)
            sync.dma_start(out=self.v_sdn[:], in_=self.ws_dp).then_inc(dL.h, 16)
            sync.dma_start(out=self.v_tmp[:], in_=self.ws_ug[:, :IC]).then_inc(dL.h, 16)
        dL.n += 8 * 16
        t_loads = dL.n

        @self.sec("vector")
        def _(vector):
            vector.wait_ge(dL.h, t_loads)
            vector.memset(self.c_ones_b[:], 1.0)
            vector.memset(self.c_ones_f[:], 1.0)
            vector.memset(self.c_eps[:], LNEPS)
            vector.tensor_scalar(self.v_sup[:], self.v_sup[:],
                                 1.0 / math.sqrt(2048.0), None, ALU.mult)
            vector.tensor_tensor(self.v_sup[:], self.v_sup[:], self.v_tmp[:], ALU.mult)
            vector.tensor_scalar(self.v_sgt[:], self.v_sgt[:],
                                 1.0 / math.sqrt(2048.0), None, ALU.mult)
            vector.tensor_scalar(self.v_sdn[:], self.v_sdn[:],
                                 1.0 / math.sqrt(8192.0), None, ALU.mult)
        self.t_consts_v = ve.n

    def _prep(self):
        dG, dS, ve = self.dG, self.dS, self.ve

        @self.sec("gpsimd")
        def _(g):
            base = dG.n
            wpr = self.wp.rearrange("(v hh r) d -> v hh r d", v=3, hh=NH)
            for v in range(3):
                g.wait_ge(dG.h, base)
                g.dma_start(out=self.wp_bf[:, v], in_=wpr[v]).then_inc(dG.h, 16)
                base += 16
            for s in range(2):
                g.wait_ge(dG.h, base)
                g.dma_start(out=self.x_bf[s], in_=self.x[s]).then_inc(dG.h, 16)
                base += 16
            for src, dst, nsplit in ((self.wo, self.wo_bf, 2),
                                     (self.wup, self.wup_bf, 4),
                                     (self.wgt, self.wgt_bf, 4)):
                rows = src.shape[0]
                step = rows // nsplit
                for k in range(nsplit):
                    g.wait_ge(dG.h, base)
                    g.dma_start(out=dst[k * step:(k + 1) * step],
                                in_=src[k * step:(k + 1) * step]).then_inc(dG.h, 16)
                    base += 16
        dG.n += 16 * 15
        self.t_cast_x = [16 * 4, 16 * 5]
        self.t_cast_wp = 16 * 3
        self.t_cast_wo = 16 * 7
        self.t_cast_wup = 16 * 11
        self.t_cast_wgt = 16 * 15

        self._wd_half = I // 2
        self._wd_prev_bfy = 0
        self._wd_prev_storeB = None
        self.t_wd_chunk = []
        self.t_wd_done = None

    def _prep_wd_chunk(self, c):
        dG, dS, ve = self.dG, self.dS, self.ve
        half = self._wd_half

        def p_load():
            prev_bfy = self._wd_prev_bfy
            t0 = dS.n

            @self.sec("scalar")
            def _(scal, c=c, prev_bfy=prev_bfy):
                scal.wait_ge(dS.h, dS.n)
                if prev_bfy > 0:
                    scal.wait_ge(ve.h, prev_bfy)
                scal.dma_start(out=self.p_wa[:],
                               in_=self.wdn[c * 128:(c + 1) * 128, 0:half]).then_inc(dS.h, 16)
                scal.dma_start(out=self.p_wb[:],
                               in_=self.wdn[c * 128:(c + 1) * 128, half:I]).then_inc(dS.h, 16)
            dS.n += 32
            self._wd_t_ld = dS.n

        def p1():
            t_ld = self._wd_t_ld

            @self.sec("vector")
            def _(v, c=c, t_ld=t_ld):
                v.wait_ge(dS.h, t_ld)
                sv = self.v_sdn[:, c:c + 1]
                v.tensor_scalar(self.p_wa[:], self.p_wa[:], sv, None, ALU.mult)
                v.tensor_scalar(self.p_wb[:], self.p_wb[:], sv, None, ALU.mult)
                v.tensor_add(self.p_wa[:], self.p_wa[:], self.p_wb[:])
                v.scalar_tensor_tensor(self.p_wb[:], self.p_wb[:], -2.0,
                                       self.p_wa[:], ALU.mult, ALU.add)

        def mk_bfy(buf):
            def p_bfy():
                @self.sec("vector")
                def _(v, buf=buf):
                    for stc in (16, 8, 4, 2, 1):
                        seg = stc * 128
                        for g0 in range(0, half, 2 * seg):
                            av = buf[:, g0:g0 + seg]
                            bv = buf[:, g0 + seg:g0 + 2 * seg]
                            v.tensor_add(av, av, bv)
                            v.scalar_tensor_tensor(bv, bv, -2.0, av,
                                                   ALU.mult, ALU.add)
            return p_bfy

        def p4():
            prev_storeB = self._wd_prev_storeB

            @self.sec("vector")
            def _(v, prev_storeB=prev_storeB):
                if prev_storeB is not None:
                    v.wait_ge(dG.h, prev_storeB)
                v.tensor_copy(self.p_wo[:], self.p_wa[:])
            t_castA = ve.n

            @self.sec("gpsimd")
            def _(g, c=c, t_castA=t_castA):
                g.wait_ge(dG.h, dG.n)
                g.wait_ge(ve.h, t_castA)
                g.dma_start(out=self.wd3_bf[c * 128:(c + 1) * 128, 0:half],
                            in_=self.p_wo[:]).then_inc(dG.h, 16)
            dG.n += 16
            self._wd_storeA = dG.n

        def p5():
            t_storeA = self._wd_storeA

            @self.sec("vector")
            def _(v, t_storeA=t_storeA):
                v.wait_ge(dG.h, t_storeA)
                v.tensor_copy(self.p_wo[:], self.p_wb[:])
            t_castB = ve.n
            self._wd_prev_bfy = ve.n

            @self.sec("gpsimd")
            def _(g, c=c, t_castB=t_castB):
                g.wait_ge(dG.h, dG.n)
                g.wait_ge(ve.h, t_castB)
                g.dma_start(out=self.wd3_bf[c * 128:(c + 1) * 128, half:I],
                            in_=self.p_wo[:]).then_inc(dG.h, 16)
            dG.n += 16
            self._wd_prev_storeB = dG.n
            self.t_wd_chunk.append(dG.n)
            if c == DC - 1:
                self.t_wd_done = dG.n

        return [p_load, p1, mk_bfy(self.p_wa), mk_bfy(self.p_wb), p4, p5]

    def _prep_passb_build(self):
        dL, dG = self.dL, self.dG
        self.t_scratch = {}
        self._passb = []
        self._land_free = 0

        def item(name, src_bf, dst_s, c, cols, t_src_dG, last):
            def emit():
                tr0 = dL.n

                @self.sec("sync")
                def _(sync):
                    sync.wait_ge(dL.h, dL.n)
                    if self._land_free > 0:
                        sync.wait_ge(dG.h, self._land_free)
                    sync.wait_ge(dG.h, self.t_wd_done if t_src_dG is None else t_src_dG)
                    sync.dma_start(out=self.p_land[:, 0:cols],
                                   in_=src_bf[:, c * 128:(c + 1) * 128],
                                   transpose=True).then_inc(dL.h, 16)
                dL.n += 16
                t_tr = dL.n

                @self.sec("gpsimd")
                def _(g, t_tr=t_tr):
                    g.wait_ge(dG.h, dG.n)
                    g.wait_ge(dL.h, t_tr)
                    g.dma_start(out=dst_s[c], in_=self.p_land[:, 0:cols]).then_inc(dG.h, 16)
                dG.n += 16
                self._land_free = dG.n
                if last:
                    self.t_scratch[name] = dG.n
            return emit

        wp_rows = self.wp_bf.rearrange("h v r d -> (h v r) d")
        for c in range(DC):
            self._passb.append(item("wp", wp_rows, self.wpT_s, c, 3 * D,
                                    self.t_cast_wp, c == DC - 1))
        for c in range(DC):
            self._passb.append(item("wo", self.wo_bf, self.woT_s, c, D,
                                    self.t_cast_wo, c == DC - 1))
        for c in range(DC):
            self._passb.append(item("wup", self.wup_bf, self.wupT_s, c, I,
                                    self.t_cast_wup, c == DC - 1))
        for c in range(DC):
            self._passb.append(item("wgt", self.wgt_bf, self.wgtT_s, c, I,
                                    self.t_cast_wgt, c == DC - 1))
        for c in range(IC):
            self._passb.append(item("wd3", self.wd3_bf, self.wd3T_s, c, D,
                                    None, c == IC - 1))


    def _drain_passb(self, k):
        for _ in range(min(k, len(self._passb))):
            self._passb.pop(0)()

    def _attn_seq(self, s, sbt):
        dL, dG, pe, sc, ve = self.dL, self.dG, self.pe, self.sc, self.ve
        ps, ptr = self.ps, self.ptr

        s_xT = sbt("s_xT%d" % s, [128, DC, SEQ], BF16)
        s_wph = sbt("s_wph%d" % s, [128, 2, DC, 3, 128], BF16)
        s_q = sbt("s_q%d" % s, [128, 2, SEQ], BF16)
        s_k = sbt("s_k%d" % s, [128, 2, SEQ], BF16)
        s_vT = sbt("s_vT%d" % s, [128, SEQ], BF16)
        s_v = sbt("s_v%d" % s, [128, 2, TKC, 128], BF16)
        s_e = sbt("s_e%d" % s, [128, TKC, TQ], BF16)
        s_den = sbt("s_den%d" % s, [1, TQ], F32)
        s_rec = sbt("s_rec%d" % s, [1, TQ], F32)
        s_bc = sbt("s_bc%d" % s, [128, TQ], F32)
        s_ob = sbt("s_ob%d" % s, [128, 2, TQ], BF16)

        wpT_r = self.wpT_s.rearrange("c p (hh v o) -> p c hh v o", hh=NH, v=3)
        t_xcast = self.t_cast_x[s]

        t_prev_phase = getattr(self, "t_y_done", 0) if s == 1 else 0

        @self.sec("sync")
        def _(sync, t_prev_phase=t_prev_phase):
            sync.wait_ge(dL.h, dL.n)
            sync.wait_ge(dG.h, t_xcast)
            if t_prev_phase > 0:
                sync.wait_ge(dG.h, t_prev_phase)
            for c in range(DC):
                sync.dma_start(out=s_xT[:, c, :],
                               in_=self.x_bf[s, :, c * 128:(c + 1) * 128],
                               transpose=True).then_inc(dL.h, 16)
        dL.n += 16 * DC
        t_xT = dL.n

        wp_rows = self.wp_bf.rearrange("h v r d -> (h v r) d")
        s_wph_f = s_wph.rearrange("p b c v o -> p b c (v o)")

        def load_wph(sync, h):
            buf = h % 2
            if s == 0 and h < 8:
                for c in range(DC):
                    sync.dma_start(out=s_wph_f[:, buf, c, :],
                                   in_=wp_rows[h * 384:(h + 1) * 384,
                                               c * 128:(c + 1) * 128],
                                   transpose=True).then_inc(dL.h, 16)
                return 16 * DC
            sync.dma_start(out=s_wph[:, buf],
                           in_=wpT_r[:, :, h]).then_inc(dL.h, 16)
            return 16

        t_wph = []

        n_h0 = [0]

        @self.sec("sync")
        def _(sync):
            sync.wait_ge(dL.h, dL.n)
            if not (s == 0 and 0 < 8):
                sync.wait_ge(dG.h, self.t_scratch["wp"])
            else:
                sync.wait_ge(dG.h, self.t_cast_wp)
            n_h0[0] = load_wph(sync, 0)
        dL.n += 16 * DC if s == 0 else 16
        t_wph.append(dL.n)

        q_evac_ve = [self.t_consts_v, self.t_consts_v]
        kv_evac_sc = [0, 0]
        attn_pe = [0, 0]
        qkv_pe = [0, 0]
        bank_free_sc = [0, 0]
        e_free_pe = 0
        oev_prev_ve = 0
        ob_st = [0, 0]
        o_evac_count = 0
        t_os_free = getattr(self, "t_wout_ols", 0)
        wd_pieces = []

        for h in range(NH):
            buf = h % 2
            t_w = t_wph[h]

            if h + 1 < NH:
                t_prev_qkv = qkv_pe[(h + 1) % 2]

                if s == 0:
                    wd_pieces.extend(self._prep_wd_chunk(h))
                    if h >= 3:
                        self._drain_passb(4)
                else:
                    if h >= 2:
                        self._drain_passb(6)

                nx = h + 1

                @self.sec("sync")
                def _(sync, h=h, nx=nx, t_prev_qkv=t_prev_qkv):
                    sync.wait_ge(dL.h, dL.n)
                    if not (s == 0 and nx < 8):
                        sync.wait_ge(dG.h, self.t_scratch["wp"])
                    if h >= 1:
                        sync.wait_ge(pe.h, t_prev_qkv)
                    load_wph(sync, nx)
                dL.n += 16 * DC if (s == 0 and nx < 8) else 16
                t_wph.append(dL.n)

            t_qe, t_ke, t_at = q_evac_ve[buf], kv_evac_sc[buf], attn_pe[buf]
            pe0, ve0, sc0 = pe.n, ve.n, sc.n

            @self.sec("tensor")
            def _(te, buf=buf, t_w=t_w, t_qe=t_qe, t_ke=t_ke, t_at=t_at,
                  ve0=ve0, sc0=sc0):
                te.wait_ge(dL.h, max(t_w, t_xT))
                te.wait_ge(pe.h, t_at)
                te.wait_ge(ve.h, t_qe)
                te.wait_ge(sc.h, t_ke)
                for jt in range(NTQ):
                    if jt > 0:
                        te.wait_ge(ve.h, ve0 + jt)
                        te.wait_ge(sc.h, sc0 + 2 * jt)
                    tq = slice(jt * TQ, (jt + 1) * TQ)
                    for c in range(DC):
                        te.matmul(ps[:, 4, :], s_wph[:, buf, c, 0, :], s_xT[:, c, tq],
                                  start=(c == 0), stop=(c == DC - 1))
                        te.matmul(ps[:, 5, :], s_wph[:, buf, c, 1, :], s_xT[:, c, tq],
                                  start=(c == 0), stop=(c == DC - 1))
                        ins = te.matmul(ps[:, 6, :], s_wph[:, buf, c, 2, :],
                                        s_xT[:, c, tq],
                                        start=(c == 0), stop=(c == DC - 1))
                    ins.then_inc(pe.h, 1)
            pe.n += NTQ

            @self.sec("vector")
            def _(v, buf=buf, pe0=pe0):
                for jt in range(NTQ):
                    tq = slice(jt * TQ, (jt + 1) * TQ)
                    v.wait_ge(pe.h, pe0 + jt + 1)
                    v.tensor_scalar(s_q[:, buf, tq], ps[:, 4, :], RS128, None,
                                    ALU.mult)
            q_evac_ve[buf] = ve.n

            @self.sec("scalar")
            def _(scal, buf=buf, pe0=pe0):
                for jt in range(NTQ):
                    tq = slice(jt * TQ, (jt + 1) * TQ)
                    scal.wait_ge(pe.h, pe0 + jt + 1)
                    scal.copy(s_k[:, buf, tq], ps[:, 5, :])
                    scal.copy(s_vT[:, tq], ps[:, 6, :])
            t_v_evac = sc.n
            qkv_pe[buf] = pe.n

            pe1, sc1 = pe.n, sc.n

            @self.sec("tensor")
            def _(te, t_v_evac=t_v_evac, sc1=sc1):
                te.wait_ge(sc.h, t_v_evac)
                for g in range(4):
                    if g > 0:
                        te.wait_ge(sc.h, sc1 + g)
                    for k in range(4):
                        cc = 4 * g + k
                        ins = te.transpose(ptr[:, k * 128:(k + 1) * 128],
                                           s_vT[:, cc * 128:(cc + 1) * 128],
                                           self.c_id[:])
                    ins.then_inc(pe.h, 1)
            pe.n += 4

            @self.sec("scalar")
            def _(scal, buf=buf, pe1=pe1):
                for g in range(4):
                    scal.wait_ge(pe.h, pe1 + g + 1)
                    scal.copy(s_v[:, buf, 4 * g:4 * g + 4, :], ptr[:])
            kv_evac_sc[buf] = sc.n

            for jt in range(NTQ):
                chunks = list(range(4 * jt, 4 * jt + 4)) + list(range(0, 4 * jt))
                n = len(chunks)
                tq = slice(jt * TQ, (jt + 1) * TQ)

                s_base = pe.n
                e_base = sc.n
                m_base = ve.n
                t_efree = e_free_pe
                t_oprev = oev_prev_ve
                bfree = list(bank_free_sc)

                @self.sec("tensor")
                def _(te, buf=buf, chunks=chunks, n=n, tq=tq, s_base=s_base,
                      e_base=e_base, m_base=m_base, t_oprev=t_oprev, bfree=bfree):
                    LAG = 3

                    def s_mm(i):
                        c = chunks[i]
                        if i >= 2:
                            te.wait_ge(sc.h, e_base + i - 1)
                        elif bfree[i % 2] > 0:
                            te.wait_ge(sc.h, bfree[i % 2])
                        te.matmul(ps[:, i % 2, :],
                                  s_k[:, buf, c * 128:(c + 1) * 128],
                                  s_q[:, buf, tq], start=True,
                                  stop=True).then_inc(pe.h, 1)

                    def acc_mm(i):
                        c = chunks[i]
                        first = (i == 0)
                        last = (i == n - 1)
                        if i < 4:
                            te.wait_ge(ve.h, m_base + i + 1)
                        else:
                            te.wait_ge(sc.h, e_base + i + 1)
                        if first and t_oprev > 0:
                            te.wait_ge(ve.h, t_oprev)
                        te.matmul(ps[0:1, 3, :], self.c_ones_b[:], s_e[:, c, :],
                                  start=first, stop=last)
                        ins = te.matmul(ps[:, 2, :], s_v[:, buf, c, :], s_e[:, c, :],
                                        start=first, stop=last)
                        if last:
                            ins.then_inc(pe.h, 1)

                    for i in range(n):
                        s_mm(i)
                        if i >= LAG:
                            acc_mm(i - LAG)
                    for i in range(max(0, n - LAG), n):
                        acc_mm(i)
                pe.n += n + 1
                t_acc = pe.n

                @self.sec("scalar")
                def _(scal, chunks=chunks, s_base=s_base, t_efree=t_efree):
                    for i, c in enumerate(chunks):
                        scal.wait_ge(pe.h, s_base + i + 1)
                        if i == 0 and t_efree > 0:
                            scal.wait_ge(pe.h, t_efree)
                        scal.activation(s_e[:, c, :], ps[:, i % 2, :],
                                        AF.Exp)
                bank_free_sc = [e_base + n - 1, e_base + n]

                @self.sec("vector")
                def _(v, chunks=chunks, e_base=e_base):
                    for i in range(4):
                        c = chunks[i]
                        v.wait_ge(sc.h, e_base + i + 1)
                        v.tensor_tensor(s_e[:, c, :], s_e[:, c, :],
                                        self.c_mask[:, i, :], ALU.mult)

                @self.sec("vector")
                def _(v, t_acc=t_acc):
                    v.wait_ge(pe.h, t_acc)
                    v.tensor_copy(s_den[:], ps[0:1, 3, :])
                    v.reciprocal(s_rec[:], s_den[:])
                t_rec = ve.n

                @self.sec("tensor")
                def _(te, t_rec=t_rec):
                    te.wait_ge(ve.h, t_rec)
                    te.matmul(ps[:, 3, :], self.c_ones_f[:], s_rec[:],
                              start=True, stop=True).then_inc(pe.h, 1)
                pe.n += 1
                t_bcmm = pe.n

                obuf = o_evac_count % 2
                t_ob_free = ob_st[obuf]

                @self.sec("vector")
                def _(v, obuf=obuf, t_bcmm=t_bcmm, t_ob_free=t_ob_free):
                    v.wait_ge(pe.h, t_bcmm)
                    v.tensor_copy(s_bc[:], ps[:, 3, :])
                    if t_ob_free > 0:
                        v.wait_ge(dG.h, t_ob_free)
                    v.tensor_tensor(s_ob[:, obuf, :], ps[:, 2, :], s_bc[:],
                                    ALU.mult)
                oev_prev_ve = ve.n
                t_oe = ve.n
                e_free_pe = t_acc

                if wd_pieces:
                    wd_pieces.pop(0)()
                    if len(wd_pieces) > 12:
                        wd_pieces.pop(0)()

                @self.sec("gpsimd")
                def _(g, h=h, jt=jt, obuf=obuf, t_oe=t_oe, t_os_free=t_os_free):
                    g.wait_ge(dG.h, dG.n)
                    g.wait_ge(ve.h, t_oe)
                    if t_os_free > 0:
                        g.wait_ge(dL.h, t_os_free)
                    g.dma_start(out=self.o_s[h * 128:(h + 1) * 128,
                                             jt * TQ:(jt + 1) * TQ],
                                in_=s_ob[:, obuf, :]).then_inc(dG.h, 16)
                dG.n += 16
                ob_st[obuf] = dG.n
                o_evac_count += 1
                t_os_free = 0
                if h == NH - 1:
                    self._o_jt_st = getattr(self, "_o_jt_st", [0] * NTQ)
                    self._o_jt_st[jt] = dG.n

            attn_pe[buf] = pe.n
            if h == NH - 1:
                self.t_attn_pe_end = pe.n
                self.t_attn_ve_end = ve.n
                self.t_attn_sc_end = sc.n

        if s == 0:
            wd_pieces.extend(self._prep_wd_chunk(DC - 1))
        while wd_pieces:
            wd_pieces.pop(0)()
        self.t_attn_o_st = dG.n
        if s == 0 and "os" in self.taps:
            t_d = dG.n

            @self.sec("gpsimd")
            def _(g, t_d=t_d):
                g.wait_ge(dG.h, t_d)
                g.dma_start(out=self.taps["os"], in_=self.o_s).then_inc(dG.h, 16)
            dG.n += 16

    def _wout_seq(self, s, sbt):
        dL, dG, pe, sc, ve = self.dL, self.dG, self.pe, self.sc, self.ve
        ps = self.ps
        NT = SEQ // 128

        s_woT = sbt("s_woT%d" % s, [128, DC, D], BF16)
        s_ol = sbt("s_ol%d" % s, [128, 2, DC, 128], BF16)
        s_xr = sbt("s_xr%d" % s, [128, 2, D], F32)
        s_r = sbt("s_r%d" % s, [128, D], F32)
        s_y = sbt("s_y%d" % s, [128, 2, D], F32)
        s_yb = sbt("s_yb%d" % s, [128, 2, D], BF16)
        s_ln = sbt("s_ln%d" % s, [128, 56], F32)

        t_o_st = self.t_attn_o_st
        o_sr = self.o_s.rearrange("(c p) t -> p c t", p=128)

        @self.sec("sync")
        def _(sync):
            sync.wait_ge(dL.h, dL.n)
            sync.wait_ge(dG.h, self.t_scratch["wo"])
            sync.wait_ge(pe.h, self.t_attn_pe_end)
            sync.dma_start(out=s_woT[:],
                           in_=self.woT_s.rearrange("c p d -> p c d")).then_inc(dL.h, 16)
        dL.n += 16
        t_woT = dL.n

        def load_ol(sync, tc):
            sync.dma_start(out=s_ol[:, tc % 2, :, :],
                           in_=o_sr[:, :, tc * 128:(tc + 1) * 128]).then_inc(dL.h, 16)

        def load_xr(sync, tc):
            sync.dma_start(out=s_xr[:, tc % 2, :],
                           in_=self.x[s, tc * 128:(tc + 1) * 128, :]).then_inc(dL.h, 16)

        @self.sec("sync")
        def _(sync):
            sync.wait_ge(dL.h, dL.n)
            sync.wait_ge(pe.h, self.t_attn_pe_end)
            sync.wait_ge(dG.h, self._o_jt_st[0])
            load_ol(sync, 0)
            load_xr(sync, 0)
        dL.n += 32
        t_ol = [dL.n]

        bank_ve = [0] * 6
        ol_read_pe = [0, 0]
        y_st = [0, 0]
        ln_done_ve = [0, 0]
        for tc in range(NT):
            buf = tc % 2
            if tc + 1 < NT:
                t_free = ol_read_pe[(tc + 1) % 2]
                t_yfree = ln_done_ve[(tc + 1) % 2]

                @self.sec("sync")
                def _(sync, tc=tc, t_free=t_free, t_yfree=t_yfree):
                    sync.wait_ge(dL.h, dL.n)
                    sync.wait_ge(dG.h, self._o_jt_st[(tc + 1) // 4])
                    if t_free > 0:
                        sync.wait_ge(pe.h, t_free)
                    if t_yfree > 0:
                        sync.wait_ge(ve.h, t_yfree)
                    load_ol(sync, tc + 1)
                    load_xr(sync, tc + 1)
                dL.n += 32
                t_ol.append(dL.n)

            pe0 = pe.n
            t_this = t_ol[tc]
            bve = list(bank_ve)

            @self.sec("tensor")
            def _(te, tc=tc, buf=buf, t_this=t_this, bve=bve):
                te.wait_ge(dL.h, max(t_this, t_woT))
                if tc == 0:
                    te.wait_ge(ve.h, self.t_attn_ve_end)
                    te.wait_ge(sc.h, self.t_attn_sc_end)
                for dt in range(4):
                    bank = (tc * 4 + dt) % 6
                    if bve[bank] > 0:
                        te.wait_ge(ve.h, bve[bank])
                    for c in range(DC):
                        ins = te.matmul(ps[:, bank, :], s_ol[:, buf, c, :],
                                        s_woT[:, c, dt * 512:(dt + 1) * 512],
                                        start=(c == 0), stop=(c == DC - 1))
                    ins.then_inc(pe.h, 1)
            pe.n += 4
            ol_read_pe[buf] = pe.n

            ve0 = ve.n

            @self.sec("vector")
            def _(v, tc=tc, buf=buf, pe0=pe0):
                for dt in range(4):
                    bank = (tc * 4 + dt) % 6
                    v.wait_ge(pe.h, pe0 + dt + 1)
                    v.tensor_tensor(s_r[:, dt * 512:(dt + 1) * 512], ps[:, bank, :],
                                    s_xr[:, buf, dt * 512:(dt + 1) * 512],
                                    ALU.add)
                for g in range(8):
                    v.bn_stats(s_ln[:, 6 * g:6 * g + 6],
                               s_r[:, 256 * g:256 * (g + 1)])
                v.bn_aggr(s_ln[:, 48:50], s_ln[:, 0:48])
            for dt in range(4):
                bank_ve[(tc * 4 + dt) % 6] = ve0 + dt + 1
            t_var = ve.n

            @self.sec("scalar")
            def _(scal, t_var=t_var):
                scal.wait_ge(ve.h, t_var)
                scal.activation(s_ln[:, 50:51], s_ln[:, 49:50], AF.Sqrt,
                                bias=self.c_eps[:])
            t_std = sc.n
            t_yst = y_st[buf]

            @self.sec("vector")
            def _(v, buf=buf, t_std=t_std, t_yst=t_yst):
                v.wait_ge(sc.h, t_std)
                v.reciprocal(s_ln[:, 51:52], s_ln[:, 50:51])
                if t_yst > 0:
                    v.wait_ge(dG.h, t_yst)
                v.tensor_scalar(s_y[:, buf, :], s_r[:], s_ln[:, 48:49], s_ln[:, 51:52],
                                ALU.subtract, ALU.mult)
            ln_done_ve[buf] = ve.n
            t_ln = ve.n

            @self.sec("scalar")
            def _(scal, buf=buf, t_ln=t_ln):
                scal.wait_ge(ve.h, t_ln)
                scal.copy(s_yb[:, buf, :], s_y[:, buf, :])
            t_yb = sc.n

            @self.sec("gpsimd")
            def _(g, tc=tc, buf=buf, t_ln=t_ln, t_yb=t_yb):
                row = s * SEQ + tc * 128
                g.wait_ge(dG.h, dG.n)
                g.wait_ge(ve.h, t_ln)
                g.dma_start(out=self.y_s[row:row + 128, :],
                            in_=s_y[:, buf, :]).then_inc(dG.h, 16)
                g.wait_ge(sc.h, t_yb)
                g.dma_start(out=self.y_bf[row:row + 128, :],
                            in_=s_yb[:, buf, :]).then_inc(dG.h, 16)
            dG.n += 32
            y_st[buf] = dG.n

        self.t_wout_ols = t_ol[-1]
        self.t_y_done = dG.n
        if s == 1 and "y" in self.taps:
            t_d = dG.n

            @self.sec("gpsimd")
            def _(g, t_d=t_d):
                g.wait_ge(dG.h, t_d)
                g.dma_start(out=self.taps["y"], in_=self.y_s).then_inc(dG.h, 16)
            dG.n += 16

    def _ffn(self, sbt):
        dL, dG, dS, pe, sc, ve = self.dL, self.dG, self.dS, self.pe, self.sc, self.ve
        ps = self.ps
        H = 512
        NT = FT // 128

        s_hsu = sbt("s_hsu", [128, DC, 128], BF16)
        s_u = sbt("s_u", [128, DC, FT], BF16)
        s_m1 = sbt("s_m1", [128, DC, FT], F32)
        s_zz = sbt("s_zz", [128, 9, FT], BF16)
        s_wu = sbt("s_wu", [128, DC, IB], BF16)
        s_wg = sbt("s_wg", [128, DC, IB], BF16)
        s_wd = sbt("s_wd", [128, IBC, D], BF16)
        s_yr = sbt("s_yr", [128, D], F32)
        s_r = sbt("s_rf", [128, D], F32)
        s_o = sbt("s_of", [128, D], F32)
        s_ln = sbt("s_lnf", [128, 56], F32)

        z_of = lambda r: s_zz[:, r, :]
        z2_of = lambda r: s_zz[:, 4 + r, :]
        sg_view = s_zz[:, 8, :]
        acc = s_m1.rearrange("p c f -> p (c f)")

        t_cv = self.t_consts_v

        @self.sec("vector")
        def _(v, t_cv=t_cv):
            v.wait_ge(ve.h, t_cv)
            for c in range(DC):
                v.tensor_scalar(s_hsu[:, c, :], self.c_h128[:],
                                self.v_suug[:, c:c + 1], None, ALU.mult)
        t_hsu = ve.n

        t_y = self.t_y_done
        u_free = 0
        epi_ve = 0
        wu_read, wg_read, wd_read = 0, 0, 0
        z2_sc_banks = [0, 0, 0, 0]
        dn_bank_ve = [0] * 4
        o_st_prev = 0
        yr_used_ve = 0

        for ft in range(NFT):
            row0 = ft * FT
            t_uf0 = u_free

            @self.sec("sync")
            def _(sync, ft=ft, row0=row0, t_uf0=t_uf0, t_gall=dG.n):
                sync.wait_ge(dL.h, dL.n)
                if ft == 0:
                    sync.wait_ge(dG.h, t_gall)
                if t_uf0 > 0:
                    sync.wait_ge(pe.h, t_uf0)
                for c in range(DC):
                    sync.dma_start(out=s_u[:, c, :],
                                   in_=self.y_bf[row0:row0 + FT, c * 128:(c + 1) * 128],
                                   transpose=True).then_inc(dL.h, 16)
            dL.n += 16 * DC
            t_yT = dL.n

            wupT_r = self.wupT_s.rearrange("c p i -> p c i")
            wgtT_r = self.wgtT_s.rearrange("c p i -> p c i")
            wd3T_r = self.wd3T_s.rearrange("c p d -> p c d")

            def load_wu(q, ib):
                q.dma_start(out=s_wu[:],
                            in_=wupT_r[:, :, ib * IB:(ib + 1) * IB]).then_inc(dL.h, 16)

            def load_wg(q, ib):
                q.dma_start(out=s_wg[:],
                            in_=wgtT_r[:, :, ib * IB:(ib + 1) * IB]).then_inc(dS.h, 16)

            def load_wd(q, ib):
                q.dma_start(out=s_wd[:],
                            in_=wd3T_r[:, ib * IBC:(ib + 1) * IBC, :]).then_inc(dL.h, 16)

            t_wua, t_wga, t_wda = wu_read, wg_read, wd_read

            @self.sec("sync")
            def _(sync, ft=ft, t_wua=t_wua, t_wda=t_wda):
                sync.wait_ge(dL.h, dL.n)
                if ft == 0:
                    sync.wait_ge(dG.h, self.t_scratch["wup"])
                    sync.wait_ge(dG.h, self.t_scratch["wd3"])
                if t_wua > 0:
                    sync.wait_ge(pe.h, t_wua)
                load_wu(sync, 0)
                sync.wait_ge(dL.h, dL.n + 16)
                if t_wda > 0:
                    sync.wait_ge(pe.h, t_wda)
                load_wd(sync, 0)
            dL.n += 32
            t_wu = [dL.n - 16]
            t_wd = [dL.n]

            @self.sec("scalar")
            def _(scal, ft=ft, t_wga=t_wga):
                scal.wait_ge(dS.h, dS.n)
                if ft == 0:
                    scal.wait_ge(dG.h, self.t_scratch["wgt"])
                    scal.wait_ge(dG.h, t_y)
                if t_wga > 0:
                    scal.wait_ge(pe.h, t_wga)
                load_wg(scal, 0)
            dS.n += 16
            t_wg = [dS.n]

            if DEBUG_FFN == "yt":
                self._dump(s_u.rearrange("p c f -> p (c f)"), DC * FT, 2)
                return

            pe0, sc0 = pe.n, sc.n
            epi_tc = list(getattr(self, "_epi_tc_read", [0] * (NT * NFT)))
            zb = list(z2_sc_banks)[:2]

            @self.sec("tensor")
            def _(te, t_yT=t_yT, t_hsu=t_hsu, sc0=sc0, zb=zb):
                te.wait_ge(dL.h, t_yT)
                te.wait_ge(ve.h, t_hsu)
                for idx in range(2 * DC):
                    hf, c = idx // DC, idx % DC
                    if idx >= 2:
                        te.wait_ge(sc.h, sc0 + idx - 1)
                    elif zb[idx % 2] > 0:
                        te.wait_ge(sc.h, zb[idx % 2])
                    te.matmul(ps[:, idx % 2, :], s_hsu[:, c, :],
                              s_u[:, c, hf * H:(hf + 1) * H],
                              start=True, stop=True).then_inc(pe.h, 1)
            pe.n += 2 * DC

            @self.sec("scalar")
            def _(scal, ft=ft, pe0=pe0, epi_tc=epi_tc):
                for idx in range(2 * DC):
                    hf, c = idx // DC, idx % DC
                    if ft > 0 and epi_tc[(ft - 1) * NT + c // 2] > 0:
                        scal.wait_ge(ve.h, epi_tc[(ft - 1) * NT + c // 2])
                    scal.wait_ge(pe.h, pe0 + idx + 1)
                    scal.copy(s_m1[:, c, hf * H:(hf + 1) * H],
                              ps[:, idx % 2, :])
            t_m1h = [sc0 + DC, sc0 + 2 * DC]

            if DEBUG_FFN == "m1":
                self._dump(s_m1.rearrange("p c f -> p (c f)"), DC * FT, 4)
                return

            t_u_half = []
            for hf in range(2):
                @self.sec("vector")
                def _(v, hf=hf, t_m1h=t_m1h):
                    v.wait_ge(sc.h, t_m1h[hf])
                    sl = slice(hf * H, (hf + 1) * H)
                    for stc in (8, 4, 2):
                        for g0 in range(0, DC, 2 * stc):
                            for k in range(stc):
                                a = s_m1[:, g0 + k, sl]
                                b = s_m1[:, g0 + stc + k, sl]
                                v.tensor_add(a, a, b)
                                v.scalar_tensor_tensor(b, b, -2.0, a,
                                                       ALU.mult, ALU.add)
                    for k in range(0, DC, 2):
                        a = s_m1[:, k, sl]
                        b = s_m1[:, k + 1, sl]
                        v.tensor_add(s_u[:, k, sl], a, b)
                        v.tensor_sub(s_u[:, k + 1, sl], a, b)
                t_u_half.append(ve.n)
            t_u = ve.n

            if DEBUG_FFN == "u":
                self._dump(s_u.rearrange("p c f -> p (c f)"), DC * FT, 2)
                return

            t_z_prev_ve = 0
            t_sg_prev_sc = 0
            for ib in range(NIB):
                t_wub, t_wgb, t_wdb = t_wu[ib], t_wg[ib], t_wd[ib]
                peu0 = pe.n
                dn_prev = list(dn_bank_ve)

                @self.sec("tensor")
                def _(te, ib=ib, t_wub=t_wub, t_wgb=t_wgb, t_u=t_u,
                      t_u_half=t_u_half,
                      t_z_prev_ve=t_z_prev_ve, t_sg_prev_sc=t_sg_prev_sc,
                      dn_prev=dn_prev, ve_z0=ve.n, sc_silu0=sc.n):
                    if ib == 0:
                        te.wait_ge(ve.h, t_u_half[0])
                    else:
                        te.wait_ge(ve.h, t_u)
                    te.wait_ge(dL.h, t_wub)
                    te.wait_ge(dS.h, t_wgb)
                    for r in range(IBC):
                        if r == 0:
                            if t_sg_prev_sc > 0:
                                te.wait_ge(sc.h, t_sg_prev_sc)
                            gate_ve = max(dn_prev + [t_z_prev_ve])
                            if gate_ve > 0:
                                te.wait_ge(ve.h, gate_ve)
                        else:
                            te.wait_ge(ve.h, ve_z0 + 2 * r)
                            te.wait_ge(sc.h, sc_silu0 + 2 * r)
                        for hf in range(2):
                            if ib == 0 and r == 0 and hf == 1:
                                te.wait_ge(ve.h, t_u_half[1])
                            for c in range(DC):
                                te.matmul(ps[:, 2 + hf, :],
                                          s_wu[:, c, r * 128:(r + 1) * 128],
                                          s_u[:, c, hf * H:(hf + 1) * H],
                                          start=(c == 0), stop=(c == DC - 1))
                                ins = te.matmul(ps[:, 4 + hf, :],
                                                s_wg[:, c, r * 128:(r + 1) * 128],
                                                s_u[:, c, hf * H:(hf + 1) * H],
                                                start=(c == 0), stop=(c == DC - 1))
                        ins.then_inc(pe.h, 1)
                pe.n += IBC
                peu1 = pe.n
                if ib == NIB - 1:
                    u_free = pe.n

                sc_silu_base = sc.n

                @self.sec("scalar")
                def _(scal, ib=ib, peu0=peu0):
                    for r in range(IBC):
                        cc = ib * IBC + r
                        scal.wait_ge(pe.h, peu0 + r + 1)
                        for hf in range(2):
                            scal.activation(sg_view[:, hf * H:(hf + 1) * H],
                                            ps[:, 4 + hf, :], AF.Silu,
                                            scale=self.v_sgt[:, cc:cc + 1])
                t_sg_prev_sc = sc.n

                ve_z_base = ve.n

                @self.sec("vector")
                def _(v, ib=ib, sc_silu_base=sc_silu_base):
                    for r in range(IBC):
                        cc = ib * IBC + r
                        for hf in range(2):
                            v.wait_ge(sc.h, sc_silu_base + 2 * r + hf + 1)
                            v.scalar_tensor_tensor(
                                z_of(r)[:, hf * H:(hf + 1) * H], ps[:, 2 + hf, :],
                                self.v_sup[:, cc:cc + 1],
                                sg_view[:, hf * H:(hf + 1) * H],
                                ALU.mult, ALU.mult)
                t_z_prev_ve = ve.n

                if DEBUG_FFN == "z" and ib == 0:
                    self._dump(s_zz.rearrange("p c f -> p (c f)")[:, 0:4 * FT], 4 * FT, 2)
                    return

                pez0 = pe.n
                scz0 = sc.n
                zb2 = list(z2_sc_banks)

                @self.sec("tensor")
                def _(te, ve_z_base=ve_z_base, zb2=zb2, scz0=scz0):
                    for idx in range(2 * IBC):
                        r, hf = idx // 2, idx % 2
                        te.wait_ge(ve.h, ve_z_base + 2 * r + hf + 1)
                        if idx >= 4:
                            te.wait_ge(sc.h, scz0 + idx - 3)
                        else:
                            if zb2[idx % 4] > 0:
                                te.wait_ge(sc.h, zb2[idx % 4])
                            if idx >= 2:
                                te.wait_ge(ve.h, ve_z_base + 2 * IBC)
                        te.matmul(ps[:, idx % 4, :], self.c_h128[:],
                                  z_of(r)[:, hf * H:(hf + 1) * H],
                                  start=True, stop=True).then_inc(pe.h, 1)
                pe.n += 2 * IBC

                @self.sec("scalar")
                def _(scal, pez0=pez0):
                    for idx in range(2 * IBC):
                        r, hf = idx // 2, idx % 2
                        scal.wait_ge(pe.h, pez0 + idx + 1)
                        scal.copy(z2_of(r)[:, hf * H:(hf + 1) * H],
                                  ps[:, idx % 4, :])
                z2_sc_banks = [sc.n - 3, sc.n - 2, sc.n - 1, sc.n]
                t_z2_all = sc.n

                if DEBUG_FFN == "z2" and ib == 0:
                    self._dump(s_zz.rearrange("p c f -> p (c f)")[:, 4 * FT:8 * FT], 4 * FT, 2)
                    return

                ped0 = pe.n
                dn_prev2 = list(dn_bank_ve)

                ved0 = ve.n

                @self.sec("tensor")
                def _(te, t_wdb=t_wdb, t_z2=t_z2_all, dn_prev2=dn_prev2, ved0=ved0):
                    te.wait_ge(sc.h, t_z2)
                    te.wait_ge(dL.h, t_wdb)
                    for k in range(NT * 4):
                        tc, dt = k // 4, k % 4
                        bank = 2 + (k % 4)
                        if k < 4:
                            if dn_prev2[k % 4] > 0:
                                te.wait_ge(ve.h, dn_prev2[k % 4])
                        else:
                            te.wait_ge(ve.h, ved0 + k - 3)
                        for r in range(IBC):
                            ins = te.matmul(ps[:, bank, :],
                                            z2_of(r)[:, tc * 128:(tc + 1) * 128],
                                            s_wd[:, r, dt * 512:(dt + 1) * 512],
                                            start=(r == 0), stop=(r == IBC - 1))
                        ins.then_inc(pe.h, 1)
                pe.n += NT * 4
                ved0 = ve.n

                @self.sec("vector")
                def _(v, ib=ib, ped0=ped0):
                    for k in range(NT * 4):
                        tc, dt = k // 4, k % 4
                        bank = 2 + (k % 4)
                        v.wait_ge(pe.h, ped0 + k + 1)
                        dst = acc[:, tc * D + dt * 512: tc * D + (dt + 1) * 512]
                        if ib == 0:
                            v.tensor_copy(dst, ps[:, bank, :])
                        else:
                            v.tensor_add(dst, dst, ps[:, bank, :])
                for k in range(NT * 4):
                    dn_bank_ve[k % 4] = ved0 + k + 1

                if ib + 1 < NIB:
                    t_um = peu1
                    ped_end = pe.n

                    @self.sec("sync")
                    def _(sync, ib=ib, t_um=t_um, ped_end=ped_end):
                        sync.wait_ge(dL.h, dL.n)
                        sync.wait_ge(pe.h, t_um)
                        load_wu(sync, ib + 1)
                        sync.wait_ge(dL.h, dL.n + 16)
                        sync.wait_ge(pe.h, ped_end)
                        load_wd(sync, ib + 1)
                    dL.n += 32
                    t_wu.append(dL.n - 16)
                    t_wd.append(dL.n)

                    @self.sec("scalar")
                    def _(scal, ib=ib, t_um=t_um):
                        scal.wait_ge(dS.h, dS.n)
                        scal.wait_ge(pe.h, t_um)
                        load_wg(scal, ib + 1)
                    dS.n += 16
                    t_wg.append(dS.n)
            wu_read, wg_read, wd_read = u_free, u_free, pe.n
            t_dn_all = ve.n

            if DEBUG_FFN == "acc":
                self._dump(acc[:, 0:DC * FT], DC * FT, 4)
                return

            for tc in range(NT):
                t_yru = yr_used_ve

                @self.sec("sync")
                def _(sync, row0=row0, tc=tc, t_yru=t_yru):
                    sync.wait_ge(dL.h, dL.n)
                    if t_yru > 0:
                        sync.wait_ge(ve.h, t_yru)
                    sync.dma_start(out=s_yr[:],
                                   in_=self.y_s[row0 + tc * 128:
                                                row0 + (tc + 1) * 128, :]).then_inc(dL.h, 16)
                dL.n += 16
                t_this = dL.n
                t_osp = o_st_prev
                veb0 = ve.n
                if not hasattr(self, "_epi_tc_read"):
                    self._epi_tc_read = [0] * (NT * NFT)
                self._epi_tc_read[ft * NT + tc] = veb0 + 1

                @self.sec("vector")
                def _(v, tc=tc, t_this=t_this, t_dn_all=t_dn_all):
                    v.wait_ge(dL.h, t_this)
                    v.wait_ge(ve.h, t_dn_all)
                    v.tensor_tensor(s_r[:], acc[:, tc * D:(tc + 1) * D],
                                    s_yr[:], ALU.add)
                    for g in range(8):
                        v.bn_stats(s_ln[:, 6 * g:6 * g + 6],
                                   s_r[:, 256 * g:256 * (g + 1)])
                    v.bn_aggr(s_ln[:, 48:50], s_ln[:, 0:48])
                yr_used_ve = veb0 + 1
                t_var = ve.n

                @self.sec("scalar")
                def _(scal, t_var=t_var):
                    scal.wait_ge(ve.h, t_var)
                    scal.activation(s_ln[:, 50:51], s_ln[:, 49:50], AF.Sqrt,
                                    bias=self.c_eps[:])
                t_std = sc.n

                @self.sec("vector")
                def _(v, t_std=t_std, t_osp=t_osp):
                    v.wait_ge(sc.h, t_std)
                    v.reciprocal(s_ln[:, 51:52], s_ln[:, 50:51])
                    if t_osp > 0:
                        v.wait_ge(dG.h, t_osp)
                    v.tensor_scalar(s_o[:], s_r[:], s_ln[:, 48:49], s_ln[:, 51:52],
                                    ALU.subtract, ALU.mult)
                t_ln = ve.n

                @self.sec("gpsimd")
                def _(g, row0=row0, tc=tc, t_ln=t_ln):
                    row = row0 + tc * 128
                    sq = row // SEQ
                    g.wait_ge(dG.h, dG.n)
                    g.wait_ge(ve.h, t_ln)
                    g.dma_start(out=self.out[sq, row % SEQ:row % SEQ + 128, :],
                                in_=s_o[:]).then_inc(dG.h, 16)
                dG.n += 16
                o_st_prev = dG.n
                epi_ve = ve.n

    def _dump(self, tile_ap, nelem, dt_bytes):
        dG, ve, sc, pe = self.dG, self.ve, self.sc, self.pe
        flat = self.out.rearrange("s t d -> (s t d)")
        dst = flat[0:128 * nelem].rearrange("(p f) -> p f", p=128)

        dL, dS = self.dL, self.dS

        @self.sec("gpsimd")
        def _(g):
            g.wait_ge(dG.h, dG.n)
            g.wait_ge(dL.h, dL.n)
            g.wait_ge(dS.h, dS.n)
            g.wait_ge(ve.h, ve.n)
            g.wait_ge(sc.h, sc.n)
            g.wait_ge(pe.h, pe.n)
            g.dma_start(out=dst, in_=tile_ap).then_inc(dG.h, 16)
        dG.n += 16


    def _finish(self):
        dL, dG, dS = self.dL, self.dG, self.dS
        if STOP_AFTER == "wout":
            t_y = self.t_y_done

            @self.sec("gpsimd")
            def _(g, t_y=t_y):
                g.wait_ge(dG.h, dG.n)
                g.wait_ge(dG.h, t_y)
                g.dma_start(out=self.out.rearrange("s t d -> (s t) d"),
                            in_=self.y_s).then_inc(dG.h, 16)
            dG.n += 16

        @self.sec("sync")
        def _(sync):
            sync.wait_ge(dG.h, dG.n)
            sync.wait_ge(dL.h, dL.n)
            sync.wait_ge(dS.h, dS.n)


_PROG_CACHE = {}


def _get_program():
    key = (STOP_AFTER, tuple(DEBUG_TAPS))
    if key not in _PROG_CACHE:
        b = Build()
        nc = b.build()
        _PROG_CACHE[key] = (nc, b)
    return _PROG_CACHE[key]


def kernel(L0_attn_x, L0_attn_w_project, L0_attn_w_out, L0_ffn_SU_ug,
           L0_ffn_SU_dp, L0_ffn_Wscale_ug, L0_ffn_Wscale_dp, L0_ffn_W_up,
           L0_ffn_W_gate, L0_ffn_W_down):
    nc, _b = _get_program()

    h128 = hadamard(128).astype(bfdt)
    ident = np.eye(128, dtype=np.float32).astype(bfdt)
    p = np.arange(128)[:, None]
    j = np.arange(TQ)[None, :]
    masks = np.stack([(j >= (r * 128 + p)).astype(np.float32) for r in range(NTQ)],
                     axis=1).astype(bfdt)

    x = np.ascontiguousarray(np.asarray(L0_attn_x, dtype=np.float32))
    com = {
        "wp": np.ascontiguousarray(np.asarray(L0_attn_w_project, dtype=np.float32)),
        "wo": np.ascontiguousarray(np.asarray(L0_attn_w_out, dtype=np.float32)),
        "wup": np.ascontiguousarray(np.asarray(L0_ffn_W_up, dtype=np.float32)),
        "wgt": np.ascontiguousarray(np.asarray(L0_ffn_W_gate, dtype=np.float32)),
        "wdn": np.ascontiguousarray(np.asarray(L0_ffn_W_down, dtype=np.float32)),
        "su_ug": np.ascontiguousarray(np.asarray(L0_ffn_SU_ug, dtype=np.float32).reshape(DC, 128).T),
        "su_dp": np.ascontiguousarray(np.asarray(L0_ffn_SU_dp, dtype=np.float32).reshape(IC, 128).T),
        "ws_ug": np.ascontiguousarray(np.asarray(L0_ffn_Wscale_ug, dtype=np.float32).reshape(2 * IC, 128).T),
        "ws_dp": np.ascontiguousarray(np.asarray(L0_ffn_Wscale_dp, dtype=np.float32).reshape(DC, 128).T),
        "h128": h128, "ident": ident, "masks": masks,
    }
    in_maps = []
    for core in range(NCORES):
        m = dict(com)
        m["x"] = np.ascontiguousarray(x[2 * core:2 * core + 2])
        in_maps.append(m)

    res = run_bass_kernel_spmd(nc, in_maps, core_ids=list(range(NCORES)))
    out = np.concatenate([np.asarray(r["out"], dtype=np.float32)
                          for r in res.results], axis=0)
    kernel._last_results = res.results
    return out.reshape(16, SEQ, D)


# version 15
# speedup vs baseline: 1.2055x; 1.0280x over previous
import math
from contextlib import ExitStack

import numpy as np
import ml_dtypes

import concourse.bass as bass
import concourse.mybir as mybir
from concourse.bass_utils import run_bass_kernel_spmd

F32 = mybir.dt.float32
BF16 = mybir.dt.bfloat16
AF = mybir.ActivationFunctionType
ALU = mybir.AluOpType
AX = mybir.AxisListType
bfdt = ml_dtypes.bfloat16

NCORES = 8
SEQ = 2048
D = 2048
DC = D // 128
NH = 16
I = 8192
IC = I // 128
TQ = 512
NTQ = SEQ // TQ
TKC = SEQ // 128
FT = 1024
NFT = 2 * SEQ // FT
IB = 512
NIB = I // IB
IBC = IB // 128
RS128 = 1.0 / math.sqrt(128.0)
LNEPS = 1e-5

DEBUG_TAPS = ()
STOP_AFTER = None
DEBUG_FFN = None


def hadamard(n):
    idx = np.arange(n)
    a = idx[:, None] & idx[None, :]
    pc = np.zeros((n, n), dtype=np.int64)
    v = a.copy()
    while v.any():
        pc += v & 1
        v >>= 1
    return np.where(pc % 2 == 0, 1.0, -1.0).astype(np.float32)


class Sem:
    def __init__(self, handle):
        self.h = handle
        self.n = 0


CHAIN_PASS = {"wait_ge", "dma_start", "nop", "engine_nop", "dma_start_transpose"}


class Chained:

    def __init__(self, raw, sem):
        self._raw = raw
        self._sem = sem

    def wait_ge(self, h, v):
        return self._raw.wait_ge(h, v)

    def dma_start(self, *a, **k):
        return self._raw.dma_start(*a, **k)

    def __getattr__(self, name):
        f = getattr(self._raw, name)
        if name.startswith("_") or not callable(f) or name in CHAIN_PASS:
            return f
        sem = self._sem
        raw = self._raw

        def wrapped(*a, **k):
            raw.wait_ge(sem.h, sem.n)
            ins = f(*a, **k)
            ins.then_inc(sem.h, 1)
            sem.n += 1
            return ins

        return wrapped


class Build:
    def __init__(self):
        self.nc = bass.Bass("TRN2", target_bir_lowering=False,
                    detect_race_conditions=True)
        self.taps = {}

    def sec(self, engine_name):
        deco = getattr(self.block, engine_name)
        if engine_name == "vector":
            return lambda fn: deco(lambda eng: fn(Chained(eng, self.ve)))
        if engine_name == "scalar":
            return lambda fn: deco(lambda eng: fn(Chained(eng, self.sc)))
        if engine_name == "gpsimd":
            return lambda fn: deco(lambda eng: fn(Chained(eng, self.gp)))
        return deco

    def build(self):
        nc = self.nc
        ein, eout = "ExternalInput", "ExternalOutput"
        self.x = nc.dram_tensor("x", [2, SEQ, D], F32, kind=ein).ap()
        self.wp = nc.dram_tensor("wp", [3 * D, D], F32, kind=ein).ap()
        self.wo = nc.dram_tensor("wo", [D, D], F32, kind=ein).ap()
        self.wup = nc.dram_tensor("wup", [I, D], F32, kind=ein).ap()
        self.wgt = nc.dram_tensor("wgt", [I, D], F32, kind=ein).ap()
        self.wdn = nc.dram_tensor("wdn", [D, I], F32, kind=ein).ap()
        self.su_ug = nc.dram_tensor("su_ug", [128, DC], F32, kind=ein).ap()
        self.su_dp = nc.dram_tensor("su_dp", [128, IC], F32, kind=ein).ap()
        self.ws_ug = nc.dram_tensor("ws_ug", [128, 2 * IC], F32, kind=ein).ap()
        self.ws_dp = nc.dram_tensor("ws_dp", [128, DC], F32, kind=ein).ap()
        self.h128 = nc.dram_tensor("h128", [128, 128], BF16, kind=ein).ap()
        self.ident = nc.dram_tensor("ident", [128, 128], BF16, kind=ein).ap()
        self.masks = nc.dram_tensor("masks", [128, NTQ, TQ], BF16, kind=ein).ap()
        self.out = nc.dram_tensor("out", [2, SEQ, D], F32, kind=eout).ap()

        dram = lambda name, shape, dt: nc.dram_tensor(name, shape, dt, kind="Internal").ap()
        self.x_bf = dram("x_bf", [2, SEQ, D], BF16)
        self.wp_bf = dram("wp_bf", [NH, 3, 128, D], BF16)
        self.wo_bf = dram("wo_bf", [D, D], BF16)
        self.wup_bf = dram("wup_bf", [I, D], BF16)
        self.wgt_bf = dram("wgt_bf", [I, D], BF16)
        self.wd3_bf = dram("wd3_bf", [D, I], BF16)
        self.wpT_s = dram("wpT_s", [DC, 128, 3 * D], BF16)
        self.woT_s = dram("woT_s", [DC, 128, D], BF16)
        self.wupT_s = dram("wupT_s", [DC, 128, I], BF16)
        self.wgtT_s = dram("wgtT_s", [DC, 128, I], BF16)
        self.wd3T_s = dram("wd3T_s", [IC, 128, D], BF16)
        self.o_s = dram("o_s", [D, SEQ], BF16)
        self.y_s = dram("y_s", [2 * SEQ, D], F32)
        self.y_bf = dram("y_bf", [2 * SEQ, D], BF16)

        for t in DEBUG_TAPS:
            shape = {"os": [D, SEQ], "y": [2 * SEQ, D]}[t]
            self.taps[t] = nc.dram_tensor("tap_" + t, shape, F32, kind=eout).ap()

        with ExitStack() as ctx:
            en = ctx.enter_context
            sbt = lambda name, shape, dt: en(nc.sbuf_tensor(name, shape, dt))
            self.dL = Sem(en(nc.semaphore("dL")))
            self.dG = Sem(en(nc.semaphore("dG")))
            self.dS = Sem(en(nc.semaphore("dS")))
            self.pe = Sem(en(nc.semaphore("pe")))
            self.gp = Sem(en(nc.semaphore("gp")))
            self.sc = Sem(en(nc.semaphore("sc")))
            self.ve = Sem(en(nc.semaphore("ve")))

            self.c_h128 = sbt("c_h128", [128, 128], BF16)
            self.c_id = sbt("c_id", [128, 128], BF16)
            self.c_mask = sbt("c_mask", [128, NTQ, TQ], BF16)
            self.c_ones_b = sbt("c_ones_b", [128, 1], BF16)
            self.c_ones_f = sbt("c_ones_f", [1, 128], F32)
            self.c_eps = sbt("c_eps", [128, 1], F32)
            self.v_sup = sbt("v_sup", [128, IC], F32)
            self.v_sgt = sbt("v_sgt", [128, IC], F32)
            self.v_sdn = sbt("v_sdn", [128, DC], F32)
            self.v_suug = sbt("v_suug", [128, DC], F32)
            self.v_tmp = sbt("v_tmp", [128, IC], F32)

            self.ps = en(nc.psum_tensor("ps", [128, 7, 512], F32))
            self.ptr = en(nc.psum_tensor("ptr", [128, 512], BF16))

            self.block = en(nc.Block())
            self._consts()
            with ExitStack() as pctx:
                psb = lambda n_, s_, d_: pctx.enter_context(nc.sbuf_tensor(n_, s_, d_))
                self.p_wa = psb("p_wa", [128, I // 2], F32)
                self.p_wb = psb("p_wb", [128, I // 2], F32)
                self.p_wo = psb("p_wo", [128, I // 2], BF16)
                self.p_land = psb("p_land", [128, I], BF16)
                self._prep()
                self._prep_passb_build()
                for s in range(2):
                    with ExitStack() as actx:
                        asb = lambda n_, s_, d_: actx.enter_context(
                            nc.sbuf_tensor(n_, s_, d_))
                        self._attn_seq(s, asb)
                    with ExitStack() as wctx:
                        wsb = lambda n_, s_, d_: wctx.enter_context(
                            nc.sbuf_tensor(n_, s_, d_))
                        self._wout_seq(s, wsb)
            self._drain_passb(10 ** 9)
            if STOP_AFTER is None:
                with ExitStack() as fctx:
                    fsb = lambda n_, s_, d_: fctx.enter_context(
                        nc.sbuf_tensor(n_, s_, d_))
                    self._ffn(fsb)
            self._finish()
        return nc

    def _consts(self):
        dL, ve = self.dL, self.ve

        @self.sec("sync")
        def _(sync):
            sync.wait_ge(dL.h, dL.n)
            sync.dma_start(out=self.c_h128[:], in_=self.h128).then_inc(dL.h, 16)
            sync.dma_start(out=self.c_id[:], in_=self.ident).then_inc(dL.h, 16)
            sync.dma_start(out=self.c_mask[:], in_=self.masks).then_inc(dL.h, 16)
            sync.dma_start(out=self.v_suug[:], in_=self.su_ug).then_inc(dL.h, 16)
            sync.dma_start(out=self.v_sup[:], in_=self.su_dp).then_inc(dL.h, 16)
            sync.dma_start(out=self.v_sgt[:], in_=self.ws_ug[:, IC:]).then_inc(dL.h, 16)
            sync.dma_start(out=self.v_sdn[:], in_=self.ws_dp).then_inc(dL.h, 16)
            sync.dma_start(out=self.v_tmp[:], in_=self.ws_ug[:, :IC]).then_inc(dL.h, 16)
        dL.n += 8 * 16
        t_loads = dL.n

        @self.sec("vector")
        def _(vector):
            vector.wait_ge(dL.h, t_loads)
            vector.memset(self.c_ones_b[:], 1.0)
            vector.memset(self.c_ones_f[:], 1.0)
            vector.memset(self.c_eps[:], LNEPS)
            vector.tensor_scalar(self.v_sup[:], self.v_sup[:],
                                 1.0 / math.sqrt(2048.0), None, ALU.mult)
            vector.tensor_tensor(self.v_sup[:], self.v_sup[:], self.v_tmp[:], ALU.mult)
            vector.tensor_scalar(self.v_sgt[:], self.v_sgt[:],
                                 1.0 / math.sqrt(2048.0), None, ALU.mult)
            vector.tensor_scalar(self.v_sdn[:], self.v_sdn[:],
                                 1.0 / math.sqrt(8192.0), None, ALU.mult)
        self.t_consts_v = ve.n

    def _prep(self):
        dG, dS, ve = self.dG, self.dS, self.ve

        @self.sec("gpsimd")
        def _(g):
            base = dG.n
            wpr = self.wp.rearrange("(v hh r) d -> v hh r d", v=3, hh=NH)
            for v in range(3):
                g.wait_ge(dG.h, base)
                g.dma_start(out=self.wp_bf[:, v], in_=wpr[v]).then_inc(dG.h, 16)
                base += 16
            for s in range(2):
                g.wait_ge(dG.h, base)
                g.dma_start(out=self.x_bf[s], in_=self.x[s]).then_inc(dG.h, 16)
                base += 16
            for src, dst, nsplit in ((self.wo, self.wo_bf, 2),
                                     (self.wup, self.wup_bf, 4),
                                     (self.wgt, self.wgt_bf, 4)):
                rows = src.shape[0]
                step = rows // nsplit
                for k in range(nsplit):
                    g.wait_ge(dG.h, base)
                    g.dma_start(out=dst[k * step:(k + 1) * step],
                                in_=src[k * step:(k + 1) * step]).then_inc(dG.h, 16)
                    base += 16
        dG.n += 16 * 15
        self.t_cast_x = [16 * 4, 16 * 5]
        self.t_cast_wp = 16 * 3
        self.t_cast_wo = 16 * 7
        self.t_cast_wup = 16 * 11
        self.t_cast_wgt = 16 * 15

        self._wd_half = I // 2
        self._wd_prev_bfy = 0
        self._wd_prev_storeB = None
        self.t_wd_chunk = []
        self.t_wd_done = None

    def _prep_wd_chunk(self, c):
        dG, dS, ve = self.dG, self.dS, self.ve
        half = self._wd_half

        def p_load():
            prev_bfy = self._wd_prev_bfy
            t0 = dS.n

            @self.sec("scalar")
            def _(scal, c=c, prev_bfy=prev_bfy):
                scal.wait_ge(dS.h, dS.n)
                if prev_bfy > 0:
                    scal.wait_ge(ve.h, prev_bfy)
                scal.dma_start(out=self.p_wa[:],
                               in_=self.wdn[c * 128:(c + 1) * 128, 0:half]).then_inc(dS.h, 16)
                scal.dma_start(out=self.p_wb[:],
                               in_=self.wdn[c * 128:(c + 1) * 128, half:I]).then_inc(dS.h, 16)
            dS.n += 32
            self._wd_t_ld = dS.n

        def p1():
            t_ld = self._wd_t_ld

            @self.sec("vector")
            def _(v, c=c, t_ld=t_ld):
                v.wait_ge(dS.h, t_ld)
                sv = self.v_sdn[:, c:c + 1]
                v.tensor_scalar(self.p_wa[:], self.p_wa[:], sv, None, ALU.mult)
                v.tensor_scalar(self.p_wb[:], self.p_wb[:], sv, None, ALU.mult)
                v.tensor_add(self.p_wa[:], self.p_wa[:], self.p_wb[:])
                v.scalar_tensor_tensor(self.p_wb[:], self.p_wb[:], -2.0,
                                       self.p_wa[:], ALU.mult, ALU.add)

        def mk_bfy(buf):
            def p_bfy():
                @self.sec("vector")
                def _(v, buf=buf):
                    for stc in (16, 8, 4, 2, 1):
                        seg = stc * 128
                        for g0 in range(0, half, 2 * seg):
                            av = buf[:, g0:g0 + seg]
                            bv = buf[:, g0 + seg:g0 + 2 * seg]
                            v.tensor_add(av, av, bv)
                            v.scalar_tensor_tensor(bv, bv, -2.0, av,
                                                   ALU.mult, ALU.add)
            return p_bfy

        def p4():
            prev_storeB = self._wd_prev_storeB

            @self.sec("vector")
            def _(v, prev_storeB=prev_storeB):
                if prev_storeB is not None:
                    v.wait_ge(dG.h, prev_storeB)
                v.tensor_copy(self.p_wo[:], self.p_wa[:])
            t_castA = ve.n

            @self.sec("gpsimd")
            def _(g, c=c, t_castA=t_castA):
                g.wait_ge(dG.h, dG.n)
                g.wait_ge(ve.h, t_castA)
                g.dma_start(out=self.wd3_bf[c * 128:(c + 1) * 128, 0:half],
                            in_=self.p_wo[:]).then_inc(dG.h, 16)
            dG.n += 16
            self._wd_storeA = dG.n

        def p5():
            t_storeA = self._wd_storeA

            @self.sec("vector")
            def _(v, t_storeA=t_storeA):
                v.wait_ge(dG.h, t_storeA)
                v.tensor_copy(self.p_wo[:], self.p_wb[:])
            t_castB = ve.n
            self._wd_prev_bfy = ve.n

            @self.sec("gpsimd")
            def _(g, c=c, t_castB=t_castB):
                g.wait_ge(dG.h, dG.n)
                g.wait_ge(ve.h, t_castB)
                g.dma_start(out=self.wd3_bf[c * 128:(c + 1) * 128, half:I],
                            in_=self.p_wo[:]).then_inc(dG.h, 16)
            dG.n += 16
            self._wd_prev_storeB = dG.n
            self.t_wd_chunk.append(dG.n)
            if c == DC - 1:
                self.t_wd_done = dG.n

        return [p_load, p1, mk_bfy(self.p_wa), mk_bfy(self.p_wb), p4, p5]

    def _prep_passb_build(self):
        dL, dG = self.dL, self.dG
        self.t_scratch = {}
        self._passb = []
        self._land_free = 0

        def item(name, src_bf, dst_s, c, cols, t_src_dG, last):
            def emit():
                tr0 = dL.n

                @self.sec("sync")
                def _(sync):
                    sync.wait_ge(dL.h, dL.n)
                    if self._land_free > 0:
                        sync.wait_ge(dG.h, self._land_free)
                    sync.wait_ge(dG.h, self.t_wd_done if t_src_dG is None else t_src_dG)
                    sync.dma_start(out=self.p_land[:, 0:cols],
                                   in_=src_bf[:, c * 128:(c + 1) * 128],
                                   transpose=True).then_inc(dL.h, 16)
                dL.n += 16
                t_tr = dL.n

                @self.sec("gpsimd")
                def _(g, t_tr=t_tr):
                    g.wait_ge(dG.h, dG.n)
                    g.wait_ge(dL.h, t_tr)
                    g.dma_start(out=dst_s[c], in_=self.p_land[:, 0:cols]).then_inc(dG.h, 16)
                dG.n += 16
                self._land_free = dG.n
                if last:
                    self.t_scratch[name] = dG.n
            return emit

        wp_rows = self.wp_bf.rearrange("h v r d -> (h v r) d")
        for c in range(DC):
            self._passb.append(item("wp", wp_rows, self.wpT_s, c, 3 * D,
                                    self.t_cast_wp, c == DC - 1))
        for c in range(DC):
            self._passb.append(item("wo", self.wo_bf, self.woT_s, c, D,
                                    self.t_cast_wo, c == DC - 1))
        for c in range(DC):
            self._passb.append(item("wup", self.wup_bf, self.wupT_s, c, I,
                                    self.t_cast_wup, c == DC - 1))
        for c in range(DC):
            self._passb.append(item("wgt", self.wgt_bf, self.wgtT_s, c, I,
                                    self.t_cast_wgt, c == DC - 1))
        for c in range(IC):
            self._passb.append(item("wd3", self.wd3_bf, self.wd3T_s, c, D,
                                    None, c == IC - 1))


    def _drain_passb(self, k):
        for _ in range(min(k, len(self._passb))):
            self._passb.pop(0)()

    def _attn_seq(self, s, sbt):
        dL, dG, pe, sc, ve = self.dL, self.dG, self.pe, self.sc, self.ve
        ps, ptr = self.ps, self.ptr

        s_xT = sbt("s_xT%d" % s, [128, DC, SEQ], BF16)
        s_wph = sbt("s_wph%d" % s, [128, 2, DC, 3, 128], BF16)
        s_q = sbt("s_q%d" % s, [128, 2, SEQ], BF16)
        s_k = sbt("s_k%d" % s, [128, 2, SEQ], BF16)
        s_vT = sbt("s_vT%d" % s, [128, SEQ], BF16)
        s_v = sbt("s_v%d" % s, [128, 2, TKC, 128], BF16)
        s_e = sbt("s_e%d" % s, [128, TKC, TQ], BF16)
        s_den = sbt("s_den%d" % s, [1, TQ], F32)
        s_rec = sbt("s_rec%d" % s, [1, TQ], F32)
        s_bc = sbt("s_bc%d" % s, [128, TQ], F32)
        s_ob = sbt("s_ob%d" % s, [128, 2, TQ], BF16)

        wpT_r = self.wpT_s.rearrange("c p (hh v o) -> p c hh v o", hh=NH, v=3)
        t_xcast = self.t_cast_x[s]

        t_prev_phase = getattr(self, "t_y_done", 0) if s == 1 else 0

        @self.sec("sync")
        def _(sync, t_prev_phase=t_prev_phase):
            sync.wait_ge(dL.h, dL.n)
            sync.wait_ge(dG.h, t_xcast)
            if t_prev_phase > 0:
                sync.wait_ge(dG.h, t_prev_phase)
            for c in range(DC):
                sync.dma_start(out=s_xT[:, c, :],
                               in_=self.x_bf[s, :, c * 128:(c + 1) * 128],
                               transpose=True).then_inc(dL.h, 16)
        dL.n += 16 * DC
        t_xT = dL.n

        wp_rows = self.wp_bf.rearrange("h v r d -> (h v r) d")
        s_wph_f = s_wph.rearrange("p b c v o -> p b c (v o)")

        def load_wph(sync, h):
            buf = h % 2
            if s == 0 and h < 8:
                for c in range(DC):
                    sync.dma_start(out=s_wph_f[:, buf, c, :],
                                   in_=wp_rows[h * 384:(h + 1) * 384,
                                               c * 128:(c + 1) * 128],
                                   transpose=True).then_inc(dL.h, 16)
                return 16 * DC
            sync.dma_start(out=s_wph[:, buf],
                           in_=wpT_r[:, :, h]).then_inc(dL.h, 16)
            return 16

        t_wph = []

        n_h0 = [0]

        @self.sec("sync")
        def _(sync):
            sync.wait_ge(dL.h, dL.n)
            if not (s == 0 and 0 < 8):
                sync.wait_ge(dG.h, self.t_scratch["wp"])
            else:
                sync.wait_ge(dG.h, self.t_cast_wp)
            n_h0[0] = load_wph(sync, 0)
        dL.n += 16 * DC if s == 0 else 16
        t_wph.append(dL.n)

        q_evac_ve = [self.t_consts_v, self.t_consts_v]
        kv_evac_sc = [0, 0]
        attn_pe = [0, 0]
        qkv_pe = [0, 0]
        bank_free_sc = [0, 0]
        e_free_pe = 0
        oev_prev_ve = 0
        ob_st = [0, 0]
        o_evac_count = 0
        t_os_free = getattr(self, "t_wout_ols", 0)
        wd_pieces = []

        for h in range(NH):
            buf = h % 2
            t_w = t_wph[h]

            if h + 1 < NH:
                t_prev_qkv = qkv_pe[(h + 1) % 2]

                if s == 0:
                    wd_pieces.extend(self._prep_wd_chunk(h))
                    if h >= 3:
                        self._drain_passb(4)
                else:
                    if h >= 2:
                        self._drain_passb(6)

                nx = h + 1

                @self.sec("sync")
                def _(sync, h=h, nx=nx, t_prev_qkv=t_prev_qkv):
                    sync.wait_ge(dL.h, dL.n)
                    if not (s == 0 and nx < 8):
                        sync.wait_ge(dG.h, self.t_scratch["wp"])
                    if h >= 1:
                        sync.wait_ge(pe.h, t_prev_qkv)
                    load_wph(sync, nx)
                dL.n += 16 * DC if (s == 0 and nx < 8) else 16
                t_wph.append(dL.n)

            t_qe, t_ke, t_at = q_evac_ve[buf], kv_evac_sc[buf], attn_pe[buf]
            pe0, ve0, sc0 = pe.n, ve.n, sc.n

            @self.sec("tensor")
            def _(te, buf=buf, t_w=t_w, t_qe=t_qe, t_ke=t_ke, t_at=t_at,
                  ve0=ve0, sc0=sc0):
                te.wait_ge(dL.h, max(t_w, t_xT))
                te.wait_ge(pe.h, t_at)
                te.wait_ge(ve.h, t_qe)
                te.wait_ge(sc.h, t_ke)
                for jt in range(NTQ):
                    if jt > 0:
                        te.wait_ge(ve.h, ve0 + jt)
                        te.wait_ge(sc.h, sc0 + 2 * jt)
                    tq = slice(jt * TQ, (jt + 1) * TQ)
                    for c in range(DC):
                        te.matmul(ps[:, 4, :], s_wph[:, buf, c, 0, :], s_xT[:, c, tq],
                                  start=(c == 0), stop=(c == DC - 1))
                        te.matmul(ps[:, 5, :], s_wph[:, buf, c, 1, :], s_xT[:, c, tq],
                                  start=(c == 0), stop=(c == DC - 1))
                        ins = te.matmul(ps[:, 6, :], s_wph[:, buf, c, 2, :],
                                        s_xT[:, c, tq],
                                        start=(c == 0), stop=(c == DC - 1))
                    ins.then_inc(pe.h, 1)
            pe.n += NTQ

            @self.sec("vector")
            def _(v, buf=buf, pe0=pe0):
                for jt in range(NTQ):
                    tq = slice(jt * TQ, (jt + 1) * TQ)
                    v.wait_ge(pe.h, pe0 + jt + 1)
                    v.tensor_scalar(s_q[:, buf, tq], ps[:, 4, :], RS128, None,
                                    ALU.mult)
            q_evac_ve[buf] = ve.n

            @self.sec("scalar")
            def _(scal, buf=buf, pe0=pe0):
                for jt in range(NTQ):
                    tq = slice(jt * TQ, (jt + 1) * TQ)
                    scal.wait_ge(pe.h, pe0 + jt + 1)
                    scal.copy(s_k[:, buf, tq], ps[:, 5, :])
                    scal.copy(s_vT[:, tq], ps[:, 6, :])
            t_v_evac = sc.n
            qkv_pe[buf] = pe.n

            pe1, sc1 = pe.n, sc.n

            @self.sec("tensor")
            def _(te, t_v_evac=t_v_evac, sc1=sc1):
                te.wait_ge(sc.h, t_v_evac)
                for g in range(4):
                    if g > 0:
                        te.wait_ge(sc.h, sc1 + g)
                    for k in range(4):
                        cc = 4 * g + k
                        ins = te.transpose(ptr[:, k * 128:(k + 1) * 128],
                                           s_vT[:, cc * 128:(cc + 1) * 128],
                                           self.c_id[:])
                    ins.then_inc(pe.h, 1)
            pe.n += 4

            @self.sec("scalar")
            def _(scal, buf=buf, pe1=pe1):
                for g in range(4):
                    scal.wait_ge(pe.h, pe1 + g + 1)
                    scal.copy(s_v[:, buf, 4 * g:4 * g + 4, :], ptr[:])
            kv_evac_sc[buf] = sc.n

            for jt in range(NTQ):
                chunks = list(range(4 * jt, 4 * jt + 4)) + list(range(0, 4 * jt))
                n = len(chunks)
                tq = slice(jt * TQ, (jt + 1) * TQ)

                s_base = pe.n
                e_base = sc.n
                m_base = ve.n
                t_efree = e_free_pe
                t_oprev = oev_prev_ve
                bfree = list(bank_free_sc)

                @self.sec("tensor")
                def _(te, buf=buf, chunks=chunks, n=n, tq=tq, s_base=s_base,
                      e_base=e_base, m_base=m_base, t_oprev=t_oprev, bfree=bfree):
                    LAG = 3

                    def s_mm(i):
                        c = chunks[i]
                        if i >= 2:
                            te.wait_ge(sc.h, e_base + i - 1)
                        elif bfree[i % 2] > 0:
                            te.wait_ge(sc.h, bfree[i % 2])
                        te.matmul(ps[:, i % 2, :],
                                  s_k[:, buf, c * 128:(c + 1) * 128],
                                  s_q[:, buf, tq], start=True,
                                  stop=True).then_inc(pe.h, 1)

                    def acc_mm(i):
                        c = chunks[i]
                        first = (i == 0)
                        last = (i == n - 1)
                        if i < 4:
                            te.wait_ge(ve.h, m_base + i + 1)
                        else:
                            te.wait_ge(sc.h, e_base + i + 1)
                        if first and t_oprev > 0:
                            te.wait_ge(ve.h, t_oprev)
                        te.matmul(ps[0:1, 3, :], self.c_ones_b[:], s_e[:, c, :],
                                  start=first, stop=last)
                        ins = te.matmul(ps[:, 2, :], s_v[:, buf, c, :], s_e[:, c, :],
                                        start=first, stop=last)
                        if last:
                            ins.then_inc(pe.h, 1)

                    for i in range(n):
                        s_mm(i)
                        if i >= LAG:
                            acc_mm(i - LAG)
                    for i in range(max(0, n - LAG), n):
                        acc_mm(i)
                pe.n += n + 1
                t_acc = pe.n

                @self.sec("scalar")
                def _(scal, chunks=chunks, s_base=s_base, t_efree=t_efree):
                    for i, c in enumerate(chunks):
                        scal.wait_ge(pe.h, s_base + i + 1)
                        if i == 0 and t_efree > 0:
                            scal.wait_ge(pe.h, t_efree)
                        scal.activation(s_e[:, c, :], ps[:, i % 2, :],
                                        AF.Exp)
                bank_free_sc = [e_base + n - 1, e_base + n]

                @self.sec("vector")
                def _(v, chunks=chunks, e_base=e_base):
                    for i in range(4):
                        c = chunks[i]
                        v.wait_ge(sc.h, e_base + i + 1)
                        v.tensor_tensor(s_e[:, c, :], s_e[:, c, :],
                                        self.c_mask[:, i, :], ALU.mult)

                @self.sec("vector")
                def _(v, t_acc=t_acc):
                    v.wait_ge(pe.h, t_acc)
                    v.tensor_copy(s_den[:], ps[0:1, 3, :])
                    v.reciprocal(s_rec[:], s_den[:])
                t_rec = ve.n

                @self.sec("tensor")
                def _(te, t_rec=t_rec):
                    te.wait_ge(ve.h, t_rec)
                    te.matmul(ps[:, 3, :], self.c_ones_f[:], s_rec[:],
                              start=True, stop=True).then_inc(pe.h, 1)
                pe.n += 1
                t_bcmm = pe.n

                obuf = o_evac_count % 2
                t_ob_free = ob_st[obuf]

                @self.sec("vector")
                def _(v, obuf=obuf, t_bcmm=t_bcmm, t_ob_free=t_ob_free):
                    v.wait_ge(pe.h, t_bcmm)
                    v.tensor_copy(s_bc[:], ps[:, 3, :])
                    if t_ob_free > 0:
                        v.wait_ge(dG.h, t_ob_free)
                    v.tensor_tensor(s_ob[:, obuf, :], ps[:, 2, :], s_bc[:],
                                    ALU.mult)
                oev_prev_ve = ve.n
                t_oe = ve.n
                e_free_pe = t_acc

                if wd_pieces:
                    wd_pieces.pop(0)()
                    if len(wd_pieces) > 12:
                        wd_pieces.pop(0)()

                @self.sec("gpsimd")
                def _(g, h=h, jt=jt, obuf=obuf, t_oe=t_oe, t_os_free=t_os_free):
                    g.wait_ge(dG.h, dG.n)
                    g.wait_ge(ve.h, t_oe)
                    if t_os_free > 0:
                        g.wait_ge(dL.h, t_os_free)
                    g.dma_start(out=self.o_s[h * 128:(h + 1) * 128,
                                             jt * TQ:(jt + 1) * TQ],
                                in_=s_ob[:, obuf, :]).then_inc(dG.h, 16)
                dG.n += 16
                ob_st[obuf] = dG.n
                o_evac_count += 1
                t_os_free = 0
                if h == NH - 1:
                    self._o_jt_st = getattr(self, "_o_jt_st", [0] * NTQ)
                    self._o_jt_st[jt] = dG.n

            attn_pe[buf] = pe.n
            if h == NH - 1:
                self.t_attn_pe_end = pe.n
                self.t_attn_ve_end = ve.n
                self.t_attn_sc_end = sc.n

        if s == 0:
            wd_pieces.extend(self._prep_wd_chunk(DC - 1))
        while wd_pieces:
            wd_pieces.pop(0)()
        self.t_attn_o_st = dG.n
        if s == 0 and "os" in self.taps:
            t_d = dG.n

            @self.sec("gpsimd")
            def _(g, t_d=t_d):
                g.wait_ge(dG.h, t_d)
                g.dma_start(out=self.taps["os"], in_=self.o_s).then_inc(dG.h, 16)
            dG.n += 16

    def _wout_seq(self, s, sbt):
        dL, dG, pe, sc, ve = self.dL, self.dG, self.pe, self.sc, self.ve
        ps = self.ps
        NT = SEQ // 128

        s_woT = sbt("s_woT%d" % s, [128, DC, D], BF16)
        s_ol = sbt("s_ol%d" % s, [128, 2, DC, 128], BF16)
        s_xr = sbt("s_xr%d" % s, [128, 2, D], F32)
        s_r = sbt("s_r%d" % s, [128, D], F32)
        s_y = sbt("s_y%d" % s, [128, 2, D], F32)
        s_yb = sbt("s_yb%d" % s, [128, 2, D], BF16)
        s_ln = sbt("s_ln%d" % s, [128, 56], F32)

        t_o_st = self.t_attn_o_st
        o_sr = self.o_s.rearrange("(c p) t -> p c t", p=128)

        @self.sec("sync")
        def _(sync):
            sync.wait_ge(dL.h, dL.n)
            sync.wait_ge(dG.h, self.t_scratch["wo"])
            sync.wait_ge(pe.h, self.t_attn_pe_end)
            sync.dma_start(out=s_woT[:],
                           in_=self.woT_s.rearrange("c p d -> p c d")).then_inc(dL.h, 16)
        dL.n += 16
        t_woT = dL.n

        def load_ol(sync, tc):
            sync.dma_start(out=s_ol[:, tc % 2, :, :],
                           in_=o_sr[:, :, tc * 128:(tc + 1) * 128]).then_inc(dL.h, 16)

        def load_xr(sync, tc):
            sync.dma_start(out=s_xr[:, tc % 2, :],
                           in_=self.x[s, tc * 128:(tc + 1) * 128, :]).then_inc(dL.h, 16)

        @self.sec("sync")
        def _(sync):
            sync.wait_ge(dL.h, dL.n)
            sync.wait_ge(pe.h, self.t_attn_pe_end)
            sync.wait_ge(dG.h, self._o_jt_st[0])
            load_ol(sync, 0)
            load_xr(sync, 0)
        dL.n += 32
        t_ol = [dL.n]

        bank_ve = [0] * 6
        ol_read_pe = [0, 0]
        y_st = [0, 0]
        ln_done_ve = [0, 0]
        for tc in range(NT):
            buf = tc % 2
            if tc + 1 < NT:
                t_free = ol_read_pe[(tc + 1) % 2]
                t_yfree = ln_done_ve[(tc + 1) % 2]

                @self.sec("sync")
                def _(sync, tc=tc, t_free=t_free, t_yfree=t_yfree):
                    sync.wait_ge(dL.h, dL.n)
                    sync.wait_ge(dG.h, self._o_jt_st[(tc + 1) // 4])
                    if t_free > 0:
                        sync.wait_ge(pe.h, t_free)
                    if t_yfree > 0:
                        sync.wait_ge(ve.h, t_yfree)
                    load_ol(sync, tc + 1)
                    load_xr(sync, tc + 1)
                dL.n += 32
                t_ol.append(dL.n)

            pe0 = pe.n
            t_this = t_ol[tc]
            bve = list(bank_ve)

            @self.sec("tensor")
            def _(te, tc=tc, buf=buf, t_this=t_this, bve=bve):
                te.wait_ge(dL.h, max(t_this, t_woT))
                if tc == 0:
                    te.wait_ge(ve.h, self.t_attn_ve_end)
                    te.wait_ge(sc.h, self.t_attn_sc_end)
                for dt in range(4):
                    bank = (tc * 4 + dt) % 6
                    if bve[bank] > 0:
                        te.wait_ge(ve.h, bve[bank])
                    for c in range(DC):
                        ins = te.matmul(ps[:, bank, :], s_ol[:, buf, c, :],
                                        s_woT[:, c, dt * 512:(dt + 1) * 512],
                                        start=(c == 0), stop=(c == DC - 1))
                    ins.then_inc(pe.h, 1)
            pe.n += 4
            ol_read_pe[buf] = pe.n

            ve0 = ve.n

            @self.sec("vector")
            def _(v, tc=tc, buf=buf, pe0=pe0):
                for dt in range(4):
                    bank = (tc * 4 + dt) % 6
                    v.wait_ge(pe.h, pe0 + dt + 1)
                    v.tensor_tensor(s_r[:, dt * 512:(dt + 1) * 512], ps[:, bank, :],
                                    s_xr[:, buf, dt * 512:(dt + 1) * 512],
                                    ALU.add)
                for g in range(8):
                    v.bn_stats(s_ln[:, 6 * g:6 * g + 6],
                               s_r[:, 256 * g:256 * (g + 1)])
                v.bn_aggr(s_ln[:, 48:50], s_ln[:, 0:48])
            for dt in range(4):
                bank_ve[(tc * 4 + dt) % 6] = ve0 + dt + 1
            t_var = ve.n

            @self.sec("scalar")
            def _(scal, t_var=t_var):
                scal.wait_ge(ve.h, t_var)
                scal.activation(s_ln[:, 50:51], s_ln[:, 49:50], AF.Sqrt,
                                bias=self.c_eps[:])
            t_std = sc.n
            t_yst = y_st[buf]

            @self.sec("vector")
            def _(v, buf=buf, t_std=t_std, t_yst=t_yst):
                v.wait_ge(sc.h, t_std)
                v.reciprocal(s_ln[:, 51:52], s_ln[:, 50:51])
                if t_yst > 0:
                    v.wait_ge(dG.h, t_yst)
                v.tensor_scalar(s_y[:, buf, :], s_r[:], s_ln[:, 48:49], s_ln[:, 51:52],
                                ALU.subtract, ALU.mult)
            ln_done_ve[buf] = ve.n
            t_ln = ve.n

            @self.sec("scalar")
            def _(scal, buf=buf, t_ln=t_ln):
                scal.wait_ge(ve.h, t_ln)
                scal.copy(s_yb[:, buf, :], s_y[:, buf, :])
            t_yb = sc.n

            @self.sec("gpsimd")
            def _(g, tc=tc, buf=buf, t_ln=t_ln, t_yb=t_yb):
                row = s * SEQ + tc * 128
                g.wait_ge(dG.h, dG.n)
                g.wait_ge(ve.h, t_ln)
                g.dma_start(out=self.y_s[row:row + 128, :],
                            in_=s_y[:, buf, :]).then_inc(dG.h, 16)
                g.wait_ge(sc.h, t_yb)
                g.dma_start(out=self.y_bf[row:row + 128, :],
                            in_=s_yb[:, buf, :]).then_inc(dG.h, 16)
            dG.n += 32
            y_st[buf] = dG.n

        self.t_wout_ols = t_ol[-1]
        self.t_y_done = dG.n
        if s == 1 and "y" in self.taps:
            t_d = dG.n

            @self.sec("gpsimd")
            def _(g, t_d=t_d):
                g.wait_ge(dG.h, t_d)
                g.dma_start(out=self.taps["y"], in_=self.y_s).then_inc(dG.h, 16)
            dG.n += 16

    def _ffn(self, sbt):
        dL, dG, dS, pe, sc, ve = self.dL, self.dG, self.dS, self.pe, self.sc, self.ve
        ps = self.ps
        H = 512
        NT = FT // 128

        s_hsu = sbt("s_hsu", [128, DC, 128], BF16)
        s_u = sbt("s_u", [128, DC, FT], BF16)
        s_m1 = sbt("s_m1", [128, DC, FT], F32)
        s_zz = sbt("s_zz", [128, 9, FT], BF16)
        s_wu = sbt("s_wu", [128, DC, IB], BF16)
        s_wg = sbt("s_wg", [128, DC, IB], BF16)
        s_wd = sbt("s_wd", [128, IBC, D], BF16)
        s_yr = sbt("s_yr", [128, D], F32)
        s_r = sbt("s_rf", [128, D], F32)
        s_o = sbt("s_of", [128, D], F32)
        s_ln = sbt("s_lnf", [128, 56], F32)

        z_of = lambda r: s_zz[:, r, :]
        z2_of = lambda r: s_zz[:, 4 + r, :]
        sg_view = s_zz[:, 8, :]
        acc = s_m1.rearrange("p c f -> p (c f)")

        t_cv = self.t_consts_v

        @self.sec("vector")
        def _(v, t_cv=t_cv):
            v.wait_ge(ve.h, t_cv)
            for c in range(DC):
                v.tensor_scalar(s_hsu[:, c, :], self.c_h128[:],
                                self.v_suug[:, c:c + 1], None, ALU.mult)
        t_hsu = ve.n

        t_y = self.t_y_done
        u_free = 0
        epi_ve = 0
        wu_read, wg_read, wd_read = 0, 0, 0
        z2_sc_banks = [0, 0, 0]
        dn_bank_ve = [0] * 4
        o_st_prev = 0
        yr_used_ve = 0

        for ft in range(NFT):
            row0 = ft * FT
            t_uf0 = u_free

            @self.sec("sync")
            def _(sync, ft=ft, row0=row0, t_uf0=t_uf0, t_gall=dG.n):
                sync.wait_ge(dL.h, dL.n)
                if ft == 0:
                    sync.wait_ge(dG.h, t_gall)
                if t_uf0 > 0:
                    sync.wait_ge(pe.h, t_uf0)
                for c in range(DC):
                    sync.dma_start(out=s_u[:, c, :],
                                   in_=self.y_bf[row0:row0 + FT, c * 128:(c + 1) * 128],
                                   transpose=True).then_inc(dL.h, 16)
            dL.n += 16 * DC
            t_yT = dL.n

            wupT_r = self.wupT_s.rearrange("c p i -> p c i")
            wgtT_r = self.wgtT_s.rearrange("c p i -> p c i")
            wd3T_r = self.wd3T_s.rearrange("c p d -> p c d")

            def load_wu(q, ib):
                q.dma_start(out=s_wu[:],
                            in_=wupT_r[:, :, ib * IB:(ib + 1) * IB]).then_inc(dL.h, 16)

            def load_wg(q, ib):
                q.dma_start(out=s_wg[:],
                            in_=wgtT_r[:, :, ib * IB:(ib + 1) * IB]).then_inc(dS.h, 16)

            def load_wd(q, ib):
                q.dma_start(out=s_wd[:],
                            in_=wd3T_r[:, ib * IBC:(ib + 1) * IBC, :]).then_inc(dL.h, 16)

            t_wua, t_wga, t_wda = wu_read, wg_read, wd_read

            @self.sec("sync")
            def _(sync, ft=ft, t_wua=t_wua, t_wda=t_wda):
                sync.wait_ge(dL.h, dL.n)
                if ft == 0:
                    sync.wait_ge(dG.h, self.t_scratch["wup"])
                    sync.wait_ge(dG.h, self.t_scratch["wd3"])
                if t_wua > 0:
                    sync.wait_ge(pe.h, t_wua)
                load_wu(sync, 0)
                sync.wait_ge(dL.h, dL.n + 16)
                if t_wda > 0:
                    sync.wait_ge(pe.h, t_wda)
                load_wd(sync, 0)
            dL.n += 32
            t_wu = [dL.n - 16]
            t_wd = [dL.n]

            @self.sec("scalar")
            def _(scal, ft=ft, t_wga=t_wga):
                scal.wait_ge(dS.h, dS.n)
                if ft == 0:
                    scal.wait_ge(dG.h, self.t_scratch["wgt"])
                    scal.wait_ge(dG.h, t_y)
                if t_wga > 0:
                    scal.wait_ge(pe.h, t_wga)
                load_wg(scal, 0)
            dS.n += 16
            t_wg = [dS.n]

            if DEBUG_FFN == "yt":
                self._dump(s_u.rearrange("p c f -> p (c f)"), DC * FT, 2)
                return

            pe0, sc0 = pe.n, sc.n
            epi_tc = list(getattr(self, "_epi_tc_read", [0] * (NT * NFT)))
            zb = list(z2_sc_banks)[:2]

            @self.sec("tensor")
            def _(te, t_yT=t_yT, t_hsu=t_hsu, sc0=sc0, zb=zb):
                te.wait_ge(dL.h, t_yT)
                te.wait_ge(ve.h, t_hsu)
                for idx in range(2 * DC):
                    hf, c = idx // DC, idx % DC
                    if idx >= 2:
                        te.wait_ge(sc.h, sc0 + idx - 1)
                    elif zb[idx % 2] > 0:
                        te.wait_ge(sc.h, zb[idx % 2])
                    te.matmul(ps[:, idx % 2, :], s_hsu[:, c, :],
                              s_u[:, c, hf * H:(hf + 1) * H],
                              start=True, stop=True).then_inc(pe.h, 1)
            pe.n += 2 * DC

            @self.sec("scalar")
            def _(scal, ft=ft, pe0=pe0, epi_tc=epi_tc):
                for idx in range(2 * DC):
                    hf, c = idx // DC, idx % DC
                    if ft > 0 and epi_tc[(ft - 1) * NT + c // 2] > 0:
                        scal.wait_ge(ve.h, epi_tc[(ft - 1) * NT + c // 2])
                    scal.wait_ge(pe.h, pe0 + idx + 1)
                    scal.copy(s_m1[:, c, hf * H:(hf + 1) * H],
                              ps[:, idx % 2, :])
            t_m1h = [sc0 + DC, sc0 + 2 * DC]

            if DEBUG_FFN == "m1":
                self._dump(s_m1.rearrange("p c f -> p (c f)"), DC * FT, 4)
                return

            t_u_half = []
            for hf in range(2):
                @self.sec("vector")
                def _(v, hf=hf, t_m1h=t_m1h):
                    v.wait_ge(sc.h, t_m1h[hf])
                    sl = slice(hf * H, (hf + 1) * H)
                    for stc in (8, 4, 2):
                        for g0 in range(0, DC, 2 * stc):
                            for k in range(stc):
                                a = s_m1[:, g0 + k, sl]
                                b = s_m1[:, g0 + stc + k, sl]
                                v.tensor_add(a, a, b)
                                v.scalar_tensor_tensor(b, b, -2.0, a,
                                                       ALU.mult, ALU.add)
                    for k in range(0, DC, 2):
                        a = s_m1[:, k, sl]
                        b = s_m1[:, k + 1, sl]
                        v.tensor_add(s_u[:, k, sl], a, b)
                        v.tensor_sub(s_u[:, k + 1, sl], a, b)
                t_u_half.append(ve.n)
            t_u = ve.n

            if DEBUG_FFN == "u":
                self._dump(s_u.rearrange("p c f -> p (c f)"), DC * FT, 2)
                return

            t_z_prev_ve = 0
            t_sg_prev_sc = 0
            for ib in range(NIB):
                t_wub, t_wgb, t_wdb = t_wu[ib], t_wg[ib], t_wd[ib]
                peu0 = pe.n
                dn_prev = list(dn_bank_ve)

                @self.sec("tensor")
                def _(te, ib=ib, t_wub=t_wub, t_wgb=t_wgb, t_u=t_u,
                      t_u_half=t_u_half,
                      t_z_prev_ve=t_z_prev_ve, t_sg_prev_sc=t_sg_prev_sc,
                      dn_prev=dn_prev, ve_z0=ve.n, sc_silu0=sc.n):
                    if ib == 0:
                        te.wait_ge(ve.h, t_u_half[0])
                    else:
                        te.wait_ge(ve.h, t_u)
                    te.wait_ge(dL.h, t_wub)
                    te.wait_ge(dS.h, t_wgb)
                    for r in range(IBC):
                        if r == 0:
                            if t_sg_prev_sc > 0:
                                te.wait_ge(sc.h, t_sg_prev_sc)
                            gate_ve = max(dn_prev + [t_z_prev_ve])
                            if gate_ve > 0:
                                te.wait_ge(ve.h, gate_ve)
                        else:
                            te.wait_ge(ve.h, ve_z0 + 2 * r)
                            te.wait_ge(sc.h, sc_silu0 + 2 * r)
                        for hf in range(2):
                            if ib == 0 and r == 0 and hf == 1:
                                te.wait_ge(ve.h, t_u_half[1])
                            for c in range(DC):
                                te.matmul(ps[:, 2 + hf, :],
                                          s_wu[:, c, r * 128:(r + 1) * 128],
                                          s_u[:, c, hf * H:(hf + 1) * H],
                                          start=(c == 0), stop=(c == DC - 1))
                                ins = te.matmul(ps[:, 4 + hf, :],
                                                s_wg[:, c, r * 128:(r + 1) * 128],
                                                s_u[:, c, hf * H:(hf + 1) * H],
                                                start=(c == 0), stop=(c == DC - 1))
                        ins.then_inc(pe.h, 1)
                pe.n += IBC
                peu1 = pe.n
                if ib == NIB - 1:
                    u_free = pe.n

                sc_silu_base = sc.n

                @self.sec("scalar")
                def _(scal, ib=ib, peu0=peu0):
                    for r in range(IBC):
                        cc = ib * IBC + r
                        scal.wait_ge(pe.h, peu0 + r + 1)
                        for hf in range(2):
                            scal.activation(sg_view[:, hf * H:(hf + 1) * H],
                                            ps[:, 4 + hf, :], AF.Silu,
                                            scale=self.v_sgt[:, cc:cc + 1])
                t_sg_prev_sc = sc.n

                ve_z_base = ve.n

                @self.sec("vector")
                def _(v, ib=ib, sc_silu_base=sc_silu_base):
                    for r in range(IBC):
                        cc = ib * IBC + r
                        for hf in range(2):
                            v.wait_ge(sc.h, sc_silu_base + 2 * r + hf + 1)
                            v.scalar_tensor_tensor(
                                z_of(r)[:, hf * H:(hf + 1) * H], ps[:, 2 + hf, :],
                                self.v_sup[:, cc:cc + 1],
                                sg_view[:, hf * H:(hf + 1) * H],
                                ALU.mult, ALU.mult)
                t_z_prev_ve = ve.n

                if DEBUG_FFN == "z" and ib == 0:
                    self._dump(s_zz.rearrange("p c f -> p (c f)")[:, 0:4 * FT], 4 * FT, 2)
                    return

                pez0 = pe.n
                scz0 = sc.n
                zb2 = list(z2_sc_banks)

                Z2B = (0, 1, 6)

                @self.sec("tensor")
                def _(te, ve_z_base=ve_z_base, zb2=zb2, scz0=scz0):
                    for idx in range(2 * IBC):
                        r, hf = idx // 2, idx % 2
                        te.wait_ge(ve.h, ve_z_base + 2 * r + hf + 1)
                        if idx >= 3:
                            te.wait_ge(sc.h, scz0 + idx - 2)
                        elif zb2[idx % 3] > 0:
                            te.wait_ge(sc.h, zb2[idx % 3])
                        te.matmul(ps[:, Z2B[idx % 3], :], self.c_h128[:],
                                  z_of(r)[:, hf * H:(hf + 1) * H],
                                  start=True, stop=True).then_inc(pe.h, 1)
                pe.n += 2 * IBC

                @self.sec("scalar")
                def _(scal, pez0=pez0):
                    for idx in range(2 * IBC):
                        r, hf = idx // 2, idx % 2
                        scal.wait_ge(pe.h, pez0 + idx + 1)
                        scal.copy(z2_of(r)[:, hf * H:(hf + 1) * H],
                                  ps[:, Z2B[idx % 3], :])
                z2_sc_banks = [sc.n - 1, sc.n, sc.n - 2]
                t_z2_all = sc.n

                if DEBUG_FFN == "z2" and ib == 0:
                    self._dump(s_zz.rearrange("p c f -> p (c f)")[:, 4 * FT:8 * FT], 4 * FT, 2)
                    return

                ped0 = pe.n
                dn_prev2 = list(dn_bank_ve)

                ved0 = ve.n

                @self.sec("tensor")
                def _(te, t_wdb=t_wdb, t_z2=t_z2_all, dn_prev2=dn_prev2, ved0=ved0):
                    te.wait_ge(sc.h, t_z2)
                    te.wait_ge(dL.h, t_wdb)
                    for k in range(NT * 4):
                        tc, dt = k // 4, k % 4
                        bank = 2 + (k % 4)
                        if k < 4:
                            if dn_prev2[k % 4] > 0:
                                te.wait_ge(ve.h, dn_prev2[k % 4])
                        else:
                            te.wait_ge(ve.h, ved0 + k - 3)
                        for r in range(IBC):
                            ins = te.matmul(ps[:, bank, :],
                                            z2_of(r)[:, tc * 128:(tc + 1) * 128],
                                            s_wd[:, r, dt * 512:(dt + 1) * 512],
                                            start=(r == 0), stop=(r == IBC - 1))
                        ins.then_inc(pe.h, 1)
                pe.n += NT * 4
                ved0 = ve.n

                @self.sec("vector")
                def _(v, ib=ib, ped0=ped0):
                    for k in range(NT * 4):
                        tc, dt = k // 4, k % 4
                        bank = 2 + (k % 4)
                        v.wait_ge(pe.h, ped0 + k + 1)
                        dst = acc[:, tc * D + dt * 512: tc * D + (dt + 1) * 512]
                        if ib == 0:
                            v.tensor_copy(dst, ps[:, bank, :])
                        else:
                            v.tensor_add(dst, dst, ps[:, bank, :])
                for k in range(NT * 4):
                    dn_bank_ve[k % 4] = ved0 + k + 1

                if ib + 1 < NIB:
                    t_um = peu1
                    ped_end = pe.n

                    @self.sec("sync")
                    def _(sync, ib=ib, t_um=t_um, ped_end=ped_end):
                        sync.wait_ge(dL.h, dL.n)
                        sync.wait_ge(pe.h, t_um)
                        load_wu(sync, ib + 1)
                        sync.wait_ge(dL.h, dL.n + 16)
                        sync.wait_ge(pe.h, ped_end)
                        load_wd(sync, ib + 1)
                    dL.n += 32
                    t_wu.append(dL.n - 16)
                    t_wd.append(dL.n)

                    @self.sec("scalar")
                    def _(scal, ib=ib, t_um=t_um):
                        scal.wait_ge(dS.h, dS.n)
                        scal.wait_ge(pe.h, t_um)
                        load_wg(scal, ib + 1)
                    dS.n += 16
                    t_wg.append(dS.n)
            wu_read, wg_read, wd_read = u_free, u_free, pe.n
            t_dn_all = ve.n

            if DEBUG_FFN == "acc":
                self._dump(acc[:, 0:DC * FT], DC * FT, 4)
                return

            for tc in range(NT):
                t_yru = yr_used_ve

                @self.sec("sync")
                def _(sync, row0=row0, tc=tc, t_yru=t_yru):
                    sync.wait_ge(dL.h, dL.n)
                    if t_yru > 0:
                        sync.wait_ge(ve.h, t_yru)
                    sync.dma_start(out=s_yr[:],
                                   in_=self.y_s[row0 + tc * 128:
                                                row0 + (tc + 1) * 128, :]).then_inc(dL.h, 16)
                dL.n += 16
                t_this = dL.n
                t_osp = o_st_prev
                veb0 = ve.n
                if not hasattr(self, "_epi_tc_read"):
                    self._epi_tc_read = [0] * (NT * NFT)
                self._epi_tc_read[ft * NT + tc] = veb0 + 1

                @self.sec("vector")
                def _(v, tc=tc, t_this=t_this, t_dn_all=t_dn_all):
                    v.wait_ge(dL.h, t_this)
                    v.wait_ge(ve.h, t_dn_all)
                    v.tensor_tensor(s_r[:], acc[:, tc * D:(tc + 1) * D],
                                    s_yr[:], ALU.add)
                    for g in range(8):
                        v.bn_stats(s_ln[:, 6 * g:6 * g + 6],
                                   s_r[:, 256 * g:256 * (g + 1)])
                    v.bn_aggr(s_ln[:, 48:50], s_ln[:, 0:48])
                yr_used_ve = veb0 + 1
                t_var = ve.n

                @self.sec("scalar")
                def _(scal, t_var=t_var):
                    scal.wait_ge(ve.h, t_var)
                    scal.activation(s_ln[:, 50:51], s_ln[:, 49:50], AF.Sqrt,
                                    bias=self.c_eps[:])
                t_std = sc.n

                @self.sec("vector")
                def _(v, t_std=t_std, t_osp=t_osp):
                    v.wait_ge(sc.h, t_std)
                    v.reciprocal(s_ln[:, 51:52], s_ln[:, 50:51])
                    if t_osp > 0:
                        v.wait_ge(dG.h, t_osp)
                    v.tensor_scalar(s_o[:], s_r[:], s_ln[:, 48:49], s_ln[:, 51:52],
                                    ALU.subtract, ALU.mult)
                t_ln = ve.n

                @self.sec("gpsimd")
                def _(g, row0=row0, tc=tc, t_ln=t_ln):
                    row = row0 + tc * 128
                    sq = row // SEQ
                    g.wait_ge(dG.h, dG.n)
                    g.wait_ge(ve.h, t_ln)
                    g.dma_start(out=self.out[sq, row % SEQ:row % SEQ + 128, :],
                                in_=s_o[:]).then_inc(dG.h, 16)
                dG.n += 16
                o_st_prev = dG.n
                epi_ve = ve.n

    def _dump(self, tile_ap, nelem, dt_bytes):
        dG, ve, sc, pe = self.dG, self.ve, self.sc, self.pe
        flat = self.out.rearrange("s t d -> (s t d)")
        dst = flat[0:128 * nelem].rearrange("(p f) -> p f", p=128)

        dL, dS = self.dL, self.dS

        @self.sec("gpsimd")
        def _(g):
            g.wait_ge(dG.h, dG.n)
            g.wait_ge(dL.h, dL.n)
            g.wait_ge(dS.h, dS.n)
            g.wait_ge(ve.h, ve.n)
            g.wait_ge(sc.h, sc.n)
            g.wait_ge(pe.h, pe.n)
            g.dma_start(out=dst, in_=tile_ap).then_inc(dG.h, 16)
        dG.n += 16


    def _finish(self):
        dL, dG, dS = self.dL, self.dG, self.dS
        if STOP_AFTER == "wout":
            t_y = self.t_y_done

            @self.sec("gpsimd")
            def _(g, t_y=t_y):
                g.wait_ge(dG.h, dG.n)
                g.wait_ge(dG.h, t_y)
                g.dma_start(out=self.out.rearrange("s t d -> (s t) d"),
                            in_=self.y_s).then_inc(dG.h, 16)
            dG.n += 16

        @self.sec("sync")
        def _(sync):
            sync.wait_ge(dG.h, dG.n)
            sync.wait_ge(dL.h, dL.n)
            sync.wait_ge(dS.h, dS.n)


_PROG_CACHE = {}


def _get_program():
    key = (STOP_AFTER, tuple(DEBUG_TAPS))
    if key not in _PROG_CACHE:
        b = Build()
        nc = b.build()
        _PROG_CACHE[key] = (nc, b)
    return _PROG_CACHE[key]


def kernel(L0_attn_x, L0_attn_w_project, L0_attn_w_out, L0_ffn_SU_ug,
           L0_ffn_SU_dp, L0_ffn_Wscale_ug, L0_ffn_Wscale_dp, L0_ffn_W_up,
           L0_ffn_W_gate, L0_ffn_W_down):
    nc, _b = _get_program()

    h128 = hadamard(128).astype(bfdt)
    ident = np.eye(128, dtype=np.float32).astype(bfdt)
    p = np.arange(128)[:, None]
    j = np.arange(TQ)[None, :]
    masks = np.stack([(j >= (r * 128 + p)).astype(np.float32) for r in range(NTQ)],
                     axis=1).astype(bfdt)

    x = np.ascontiguousarray(np.asarray(L0_attn_x, dtype=np.float32))
    com = {
        "wp": np.ascontiguousarray(np.asarray(L0_attn_w_project, dtype=np.float32)),
        "wo": np.ascontiguousarray(np.asarray(L0_attn_w_out, dtype=np.float32)),
        "wup": np.ascontiguousarray(np.asarray(L0_ffn_W_up, dtype=np.float32)),
        "wgt": np.ascontiguousarray(np.asarray(L0_ffn_W_gate, dtype=np.float32)),
        "wdn": np.ascontiguousarray(np.asarray(L0_ffn_W_down, dtype=np.float32)),
        "su_ug": np.ascontiguousarray(np.asarray(L0_ffn_SU_ug, dtype=np.float32).reshape(DC, 128).T),
        "su_dp": np.ascontiguousarray(np.asarray(L0_ffn_SU_dp, dtype=np.float32).reshape(IC, 128).T),
        "ws_ug": np.ascontiguousarray(np.asarray(L0_ffn_Wscale_ug, dtype=np.float32).reshape(2 * IC, 128).T),
        "ws_dp": np.ascontiguousarray(np.asarray(L0_ffn_Wscale_dp, dtype=np.float32).reshape(DC, 128).T),
        "h128": h128, "ident": ident, "masks": masks,
    }
    in_maps = []
    for core in range(NCORES):
        m = dict(com)
        m["x"] = np.ascontiguousarray(x[2 * core:2 * core + 2])
        in_maps.append(m)

    res = run_bass_kernel_spmd(nc, in_maps, core_ids=list(range(NCORES)))
    out = np.concatenate([np.asarray(r["out"], dtype=np.float32)
                          for r in res.results], axis=0)
    kernel._last_results = res.results
    return out.reshape(16, SEQ, D)
